# version 10
# baseline (speedup 1.0000x reference)
import numpy as np
import concourse.bass as bass
import concourse.mybir as mybir
from concourse.bass_utils import run_bass_kernel_spmd
from concourse.bass_core_rust import IndirectOffsetOnAxis

F32 = mybir.dt.float32
BF16 = mybir.dt.bfloat16
I32 = mybir.dt.int32
AF = mybir.ActivationFunctionType
OP = mybir.AluOpType
AX = mybir.AxisListType

B, S, HID = 2, 4096, 2048
SC = 128.0 ** -0.5
C = 32
NCH = S // C
SEG = 128
NSEG = S // SEG
CPS = SEG // C
NCORE = 8
TOK = B * S
OWN = TOK // NCORE
NTC = OWN // 128
INNER = 8192
RND = float(3 << 22)
DBG_CHUNK = 0
LNS = float(np.log(SC))
ISQH = float(HID) ** -0.5
ISQI = float(INNER) ** -0.5


def _consts():
    j = np.arange(128)
    taui, pi = j // 4, j % 4
    pb, taub = j // 32, j % 32
    hm = pi[:, None] == pi[None, :]
    gkk = (hm & (taui[:, None] < taui[None, :])).astype(np.float32) * (SC * SC)
    gq = (taub[:, None] <= np.arange(32)[None, :]).astype(np.float32) * SC
    pbb = pb[:, None] == pb[None, :]
    M1c = (pbb * ((taub[:, None] <= taub[None, :]).astype(np.float32)
                  - (taub[:, None] <= 15))).astype(np.float32)
    M5 = (pbb * ((taub[:, None] == taub[None, :]).astype(np.float32)
                 - (taub[:, None] <= 15))).astype(np.float32)
    M4 = (pbb * (taub[:, None] <= taub[None, :])).astype(np.float32)
    permib = ((pb[:, None] == pi[None, :]) &
              (taub[:, None] == taui[None, :])).astype(np.float32)
    indC = np.zeros((128, 4), np.float32)
    indE = np.zeros((128, 4), np.float32)
    for p in range(4):
        indC[32 * p:32 * p + 32, p] = 1.0
        indE[32 * p + 16:32 * p + 32, p] = 1.0
    selb = np.zeros((4, 4, 128, 128), np.float32)
    for v4 in range(4):
        for p in range(4):
            for ji in range(128):
                if ji % 4 == p:
                    selb[v4, p, 32 * v4 + (ji // 4), ji] = 1.0
    selb = np.ascontiguousarray(
        selb.reshape(16, 128, 128).transpose(1, 0, 2))
    ident = np.eye(128, dtype=np.float32)
    return dict(gkk_mask=gkk, gq_mask=gq, M1c=M1c, M5=M5, M4=M4,
                PERMib=permib, PERMbi=np.ascontiguousarray(permib.T),
                indC=indC, indE=indE, selb=selb, ident=ident,
                ones_col=np.ones((128, 1), np.float32),
                ones_row=np.ones((1, 128), np.float32),
                lns_col=np.full((128, 1), np.log(128.0 ** -0.5), np.float32))


CONSTS = {k: v.astype(np.float32) for k, v in _consts().items()}


def _core_host_inputs(d, c):
    h0 = 2 * c
    cs = slice(128 * h0, 128 * (h0 + 2))
    cw = lambda w: np.ascontiguousarray(
        w[cs, 0].reshape(2, 128, 4).transpose(1, 0, 2))
    cb = lambda b: np.ascontiguousarray(b[cs].reshape(2, 128).T)
    b0 = c // 4
    seq0 = 1024 * (c % 4)
    gi = np.zeros((NTC, NCORE, 2, 128), np.int32)
    for tc in range(NTC):
        for src in range(NCORE):
            for ht in range(2):
                p = 2 * b0 + ht
                gi[tc, src, ht] = ((src * 4 + p) * S + seq0 + 128 * tc
                                   + np.arange(128))
    gi = np.ascontiguousarray(gi.reshape(-1, 128).T)
    io = dict(
        x_own=np.ascontiguousarray(d['L0_attn_x'].reshape(TOK, HID)
                                   [OWN * c:OWN * (c + 1)]),
        Wq=np.ascontiguousarray(d['L0_attn_Wq'][:, cs]),
        Wk=np.ascontiguousarray(d['L0_attn_Wk'][:, cs]),
        Wv=np.ascontiguousarray(d['L0_attn_Wv'][:, cs]),
        Wag=np.ascontiguousarray(np.concatenate(
            [d['L0_attn_Wa'][:, cs], d['L0_attn_Wg'][:, cs]], 1)),
        Wb=np.ascontiguousarray(d['L0_attn_Wb'][:, h0:h0 + 2]),
        agb_row=np.ascontiguousarray(np.concatenate(
            [d['L0_attn_ba'][cs], np.zeros(256, np.float32)])[None, :]),
        bb_row=np.ascontiguousarray(d['L0_attn_bb'][None, h0:h0 + 2]),
        cqw=cw(d['L0_attn_cqw']), cqb=cb(d['L0_attn_cqb']),
        ckw=cw(d['L0_attn_ckw']), ckb=cb(d['L0_attn_ckb']),
        cvw=cw(d['L0_attn_cvw']), cvb=cb(d['L0_attn_cvb']),
        lnw_rep=np.ascontiguousarray(np.repeat(d['L0_attn_ln_w'][None], 128, 0)),
        lnb_rep=np.ascontiguousarray(np.repeat(d['L0_attn_ln_b'][None], 128, 0)),
        b1_rep=np.ascontiguousarray(np.repeat(d['L0_ffn_b1'][None], 128, 0)),
        b2_rep=np.ascontiguousarray(np.repeat(d['L0_ffn_b2'][None], 128, 0)),
        Wo=np.ascontiguousarray(d['L0_attn_Wo']),
        w1T=np.ascontiguousarray(d['L0_ffn_w1'].T),
        w2T=np.ascontiguousarray(d['L0_ffn_w2'].T),
        w1sl=np.ascontiguousarray(d['L0_ffn_w1'].T[:, 1024 * c:1024 * (c + 1)]),
        w2sl=np.ascontiguousarray(d['L0_ffn_w2'].T[1024 * c:1024 * (c + 1)]),
        og_idx=gi,
    )
    io.update(CONSTS)
    return io


def build_nc(debug=False):
    nc = bass.Bass("TRN2", target_bir_lowering=False)
    EI = dict(kind="ExternalInput")
    dram = {}
    shapes = dict(x_own=[OWN, HID], Wq=[HID, 256], Wk=[HID, 256],
                  Wv=[HID, 256], Wag=[HID, 512], Wb=[HID, 2],
                  agb_row=[1, 512], bb_row=[1, 2],
                  cqw=[128, 2, 4], cqb=[128, 2], ckw=[128, 2, 4],
                  ckb=[128, 2], cvw=[128, 2, 4], cvb=[128, 2],
                  lnw_rep=[128, 128], lnb_rep=[128, 128],
                  b1_rep=[128, INNER], b2_rep=[128, HID],
                  Wo=[HID, HID], w1T=[HID, INNER], w2T=[INNER, HID],
                  w1sl=[HID, 1024], w2sl=[1024, HID],
                  og_idx=[128, NTC * NCORE * 2])
    for k, sh in shapes.items():
        dram[k] = nc.dram_tensor(k, sh, I32 if k == "og_idx" else F32, **EI)
    for k, v in CONSTS.items():
        dram[k] = nc.dram_tensor(k, list(v.shape), F32, **EI)
    out_t = nc.dram_tensor("out", [OWN, HID], F32, kind="ExternalOutput")
    dbg_names = ("Kint_c", "Qint_c", "nla_c", "Rt_blk", "E1_blk", "E2_blk",
                 "AC_blk", "Af_blk", "Ainv_int", "Rt_int", "Q_int", "P_blk",
                 "o_blk", "Ktok_blk", "Sneg0", "S0T0", "bpos_int", "gbf_c",
                 "t1_blk", "t2_blk")
    dbg_t = {}
    if debug:
        for k in dbg_names:
            dbg_t[k] = nc.dram_tensor("dbg_" + k, [128, 128], F32,
                                      kind="ExternalOutput")

    cc_xt_in = nc.dram_tensor("cc_xt_in", [16, 128, OWN], BF16)
    cc_xt_out = nc.dram_tensor("cc_xt_out", [NCORE, 16, 128, OWN], BF16,
                               addr_space="Shared")
    cc_og_in = nc.dram_tensor("cc_og_in", [4, S, 128], BF16)
    cc_og_out = nc.dram_tensor("cc_og_out", [NCORE * 4 * S, 128], BF16,
                               addr_space="Shared")
    cc_sw_in = nc.dram_tensor("cc_sw_in", [1, 2], F32)
    cc_sw_out = nc.dram_tensor("cc_sw_out", [1, 2], F32)
    kint_sp = nc.dram_tensor("kint_sp", [NSEG, 128, 512], BF16)
    qint_sp = nc.dram_tensor("qint_sp", [NSEG, 128, 512], BF16)
    vch_sp = nc.dram_tensor("vch_sp", [NSEG, 128, 512], BF16)
    nlab_sp = nc.dram_tensor("nlab_sp", [NSEG, 128, 512], F32)
    gb_sp = nc.dram_tensor("gb_sp", [NSEG, 128, 512], BF16)
    x1_sp = nc.dram_tensor("x1_sp", [NTC, 128, HID], F32)
    wo_bf = nc.dram_tensor("wo_bf", [HID, HID], BF16)
    w2s_sp = nc.dram_tensor("w2s_sp", [64, 128, HID], BF16)
    hq_sp = nc.dram_tensor("hq_sp", [NTC, 128, 64, 128], BF16)
    g2_sp = nc.dram_tensor("g2_sp", [NTC, 16, 128, 512], BF16)

    sb = {}

    def SBal(name, shape, dtype=F32):
        sb[name] = nc.alloc_sbuf_tensor(name, list(shape), dtype).ap()
        return sb[name]

    for k in CONSTS:
        sb[k] = nc.alloc_sbuf_tensor("sb_" + k, list(CONSTS[k].shape),
                                     F32).ap()
    SBal("ident_bf", [128, 128], BF16)
    SBal("permbi_bf", [128, 128], BF16)
    SBal("onesr_bf", [1, 128], BF16)
    SBal("agb_bf", [1, 512], BF16)
    SBal("bb_bf", [1, 2], BF16)
    SBal("lnw_s", [128, 128], F32)
    SBal("lnb_s", [128, 128], F32)
    for k in ("cqw", "ckw", "cvw"):
        SBal(k + "_s", [128, 2, 4], F32)
    for k in ("cqb", "ckb", "cvb"):
        SBal(k + "_s", [128, 2], F32)
    SBal("swk", [128, 2], F32)
    SBal("ogi_s", [128, NTC * NCORE * 2], I32)
    for p in range(4):
        SBal(f"Sneg{p}", [128, 128], F32)
        SBal(f"S0T{p}", [128, 128], BF16)
    for k in ("Rt_blk", "t1_blk", "t2_blk", "o_blk", "cvt0", "cvt1",
              "cvt2", "cvt3", "g_c"):
        SBal(k, [128, 128], F32)
    for k in ("E1_blk", "E2_blk", "AC_blk", "Af_blk", "Ainv_int", "nla_c"):
        SBal(k + "0", [128, 128], F32)
        SBal(k + "1", [128, 128], F32)
    for k in ("Rt_int", "Q_int", "P_blk", "og_blk"):
        SBal(k, [128, 128], BF16)
    for k in ("Gblk", "Ktok_blk", "Kint_c", "Qint_c", "gbf_c"):
        SBal(k + "0", [128, 128], BF16)
        SBal(k + "1", [128, 128], BF16)
    SBal("GqT0", [128, 32], BF16)
    SBal("GqT1", [128, 32], BF16)
    for k in ("ACf_c", "ACe_c"):
        SBal(k + "0", [128, 4], F32)
        SBal(k + "1", [128, 4], F32)
    for k in ("mu_c", "sq_c", "va_c", "rstd_c", "habs", "hssq", "tka", "tkb",
              "kq_c", "ysc_c", "nhz_c", "sw_par", "sw_par2",
              "mu_c2", "sq_c2", "va_c2", "rstd_c2", "tka1", "tkb1"):
        SBal(k, [128, 1], F32)
    for k in ("bpos_int", "btn_int"):
        SBal(k + "0", [128, 1], F32)
        SBal(k + "1", [128, 1], F32)
    SBal("nlb_all", [128, NSEG, 2, 2], F32)
    SBal("tails", [128, 40], BF16)
    SBal("zsum", [128, NTC], F32)
    SBal("ysc1_all", [128, NTC], F32)
    for k in ("habs8", "hssq8", "ysc2_all", "hz_all"):
        SBal(k, [128, NTC], F32)
    SBal("swv", [1, 2], F32)

    AB = SBal("arena_bf", [128, 68 * 1024], BF16)
    AFR = SBal("arena_f", [128, 7 * 1024], F32)

    class Arena:
        def __init__(self, ap, total):
            self.ap, self.total, self.off = ap, total, 0

        def take(self, n):
            a = self.ap[:, self.off:self.off + n]
            self.off += n
            assert self.off <= self.total, f"arena overflow {self.off}"
            return a

    r3 = lambda ap, d1: ap.rearrange("p (a b) -> p a b", a=d1)
    r4 = lambda ap, d1, d2: ap.rearrange("p (a b c) -> p a b c", a=d1, b=d2)

    a1b = Arena(AB, 68 * 1024)
    a1f = Arena(AFR, 7 * 1024)
    Wq_s = r3(a1b.take(4096), 16)
    Wk_s = r3(a1b.take(4096), 16)
    Wv_s = r3(a1b.take(4096), 16)
    Wag_s = r3(a1b.take(8192), 16)
    Wb_s = r3(a1b.take(32), 16)
    xT_seg = r4(a1b.take(4096), 2, 16)
    xT_segB = r4(a1b.take(4096), 2, 16)
    xb_t = a1b.take(2048)
    xTst = r3(a1b.take(2048), 16)
    uextt = r3(a1b.take(12 * 131), 12)
    Kint_seg = r4(a1b.take(512), CPS, 32)
    Qint_seg = r4(a1b.take(512), CPS, 32)
    Vch_seg = r4(a1b.take(512), 2, 2)
    Vch_segB = r4(a1b.take(512), 2, 2)
    nla_tok = r4(a1f.take(512), 2, 2)
    g_tok = r4(a1f.take(512), 2, 2)
    p3b = Arena(AB, 68 * 1024)
    p3f = Arena(AFR, 7 * 1024)
    og_own = p3b.take(2048)
    ogT = r3(p3b.take(2048), 16)
    Wo_band = r3(p3b.take(8192), 16)
    xqT_all = r4(p3b.take(16384), 16, NTC)
    w1s = r3(p3b.take(8192), 16)
    wldb = p3b.take(2048)
    hgel = p3b.take(512)
    hq_t = p3b.take(512)
    w2s_band = r3(p3b.take(8192), 16)
    xq_bf = p3b.take(2048)
    b1rep_bf = p3b.take(INNER)
    b2rep_bf = p3b.take(HID)
    hstg = p3b.take(512)
    xr = p3f.take(2048)
    x1t = p3f.take(2048)
    ytmp = p3f.take(2048)
    htf = p3f.take(512)
    htf2 = p3f.take(512)

    _OG, _OGT, _WOB, _XQT, _W1S, _WLDB = 0, 2048, 4096, 12288, 28672, 36864
    _XQBF, _B1R = 48128, 50176
    assert p3b.off == 60928, p3b.off
    og_ownB = AB[:, 60928:62976]
    ogTB = r3(AB[:, 62976:65024], 16)
    hstgB = AB[:, 65024:65536]
    ytmpB = AB[:, 65536:69632].bitcast(F32)
    Wo_bandB = r3(AB[:, _W1S:_W1S + 8192], 16)
    xrB = AB[:, _B1R:_B1R + 4096].bitcast(F32)
    w1sB = r3(AB[:, _WOB:_WOB + 8192], 16)
    wldbB = AB[:, _XQBF:_XQBF + 2048]
    hq_stg = [r3(AB[:, _XQT:_XQT + 8192], 64),
              r3(AB[:, _XQT + 8192:_XQT + 16384], 64)]
    w2s_bandB = r3(AB[:, _W1S:_W1S + 8192], 16)

    psumh = [nc.alloc_psum_tensor(f"ps{i}", [128, 512], F32) for i in range(8)]
    ps = [t.ap() for t in psumh]
    psW = ps[0][:, 0:128]
    psG = ps[1][:, 0:128]
    psGq = ps[1][:, 128:160]
    psNC = ps[1][:, 160:164]
    psNE = ps[1][:, 164:168]
    psBI = ps[1][:, 168:169]
    psSW = ps[1][:, 169:171]
    psL1 = ps[2][:, 0:128]
    psL5 = ps[2][:, 128:256]
    psL4 = ps[2][:, 256:384]
    psNI = ps[2][:, 384:512]
    psR0 = ps[6][:, 0:128]
    psQ0 = ps[6][:, 128:256]
    psVt = ps[3][:, 256:384]
    psKt = ps[3][:, 384:512]
    psRI = ps[4][:, 0:128]
    psPB = ps[4][:, 128:256]
    psGP = ps[4][:, 256:384]
    psST = ps[4][:, 384:512]
    psPK = ps[5]
    psA = ps[6]
    psB_ = ps[7]
    psbf = [ps[i][:, 0:64].bitcast(BF16) for i in range(8)]
    psVtb = ps[3][:, 256:320].bitcast(BF16)
    psKtb = ps[3][:, 384:448].bitcast(BF16)

    class Plan:
        ENGS = ("sync", "gpsimd", "tensor", "vector", "scalar")

        def __init__(self):
            self.ops = {e: [] for e in self.ENGS}
            self.cnt = {e: 0 for e in self.ENGS}
            self.dma = {e: 0 for e in ("sync", "gpsimd", "scalar")}

        def emit(self, eng, fn, waits=(), tick=False, dma=False):
            if eng in ("vector", "scalar") and not dma:
                tick = True
            inc = None
            if tick:
                self.cnt[eng] += 1
                inc = ("p" + eng, 1)
            if dma:
                self.dma[eng] += 16
                inc = ("d" + eng, 16)
            self.ops[eng].append(
                (fn, tuple(w for w in waits if w is not None), inc))

        def t(self, eng):
            return ("p" + eng, self.cnt[eng])

        def d(self, eng):
            return ("d" + eng, self.dma[eng])

    P = Plan()
    E = P.emit

    def dmov(eng, dst, src, waits=()):
        E(eng, lambda e, dst=dst, src=src: e.dma_start(out=dst, in_=src),
          waits=waits, dma=True)

    for k in CONSTS:
        dmov("sync", sb[k], dram[k].ap())
    dmov("sync", sb["lnw_s"], dram["lnw_rep"].ap())
    dmov("sync", sb["lnb_s"], dram["lnb_rep"].ap())
    for k in ("cqw", "ckw", "cvw", "cqb", "ckb", "cvb"):
        dmov("sync", sb[k + "_s"], dram[k].ap())
    dmov("sync", sb["ogi_s"], dram["og_idx"].ap())
    dmov("gpsimd", sb["ident_bf"], dram["ident"].ap())
    dmov("gpsimd", sb["permbi_bf"], dram["PERMbi"].ap())
    dmov("gpsimd", sb["onesr_bf"], dram["ones_row"].ap())
    dmov("gpsimd", sb["agb_bf"], dram["agb_row"].ap())
    dmov("gpsimd", sb["bb_bf"], dram["bb_row"].ap())
    for dst, src in ((Wq_s, "Wq"), (Wk_s, "Wk"), (Wv_s, "Wv"),
                     (Wag_s, "Wag"), (Wb_s, "Wb")):
        dmov("gpsimd", dst, dram[src].ap().rearrange("(n p) c -> p n c", p=128))
    w_lds = P.d("sync")
    w_ldg = P.d("gpsimd")

    for p in range(4):
        E("vector", lambda e, p=p: e.memset(sb[f"Sneg{p}"], 0.0), tick=True)
        E("vector", lambda e, p=p: e.memset(sb[f"S0T{p}"], 0.0), tick=True)
    E("vector", lambda e: e.memset(sb["Q_int"], 0.0), tick=True)
    E("vector", lambda e: e.memset(sb["tails"], 0.0), waits=[w_ldg], tick=True)

    x_own_r = dram["x_own"].ap().rearrange("(n p) d -> n p d", p=128)
    for i in range(NTC):
        dmov("gpsimd", xb_t, x_own_r[i], waits=[P.t("scalar"), w_lds])
        wld = P.d("gpsimd")
        for dd in range(16):
            E("tensor", lambda e, dd=dd:
              e.transpose(psbf[dd % 2], xb_t[:, 128 * dd:128 * (dd + 1)],
                          sb["ident_bf"]),
              waits=[wld, P.t("scalar")], tick=True)
            E("scalar", lambda e, dd=dd:
              e.activation(out=xTst[:, dd, :], in_=psbf[dd % 2],
                           func=AF.Copy),
              waits=[P.t("tensor"), P.d("sync") if dd == 0 else None],
              tick=True)
        dmov("sync", cc_xt_in.ap()[:, :, 128 * i:128 * (i + 1)]
             .transpose([1, 0, 2]), xTst, waits=[P.t("scalar")])
    E("gpsimd", lambda e:
      e.collective_compute("AllGather", OP.bypass,
                           replica_groups=[list(range(NCORE))],
                           ins=[cc_xt_in.ap().opt()],
                           outs=[cc_xt_out.ap().opt()]),
      waits=[P.d("sync"), P.d("gpsimd")], tick=True)
    w_agx = P.t("gpsimd")

    w1sl_r = dram["w1sl"].ap().rearrange("(n p) c -> n p c", p=128)
    w2sl_r = dram["w2sl"].ap().rearrange("(n p) c -> n p c", p=128)
    for wr, nt, fd, dst in ((w1sl_r, 16, 1024, "sw_par"),
                            (w2sl_r, 8, HID, "sw_par2")):
        first = True
        for k in range(nt):
            for q in range(fd // 128):
                dmov("gpsimd", sb["t1_blk"], wr[k][:, 128 * q:128 * (q + 1)],
                     waits=[P.t("vector"), w_agx])
                wld = P.d("gpsimd")
                E("vector", lambda e:
                  e.tensor_reduce(sb["tka"], sb["t1_blk"], AX.X, OP.add,
                                  apply_absolute_value=True),
                  waits=[wld], tick=True)
                if first:
                    E("vector", lambda e, dst=dst:
                      e.tensor_copy(out=sb[dst], in_=sb["tka"]), tick=True)
                    first = False
                else:
                    E("vector", lambda e, dst=dst:
                      e.tensor_tensor(out=sb[dst], in0=sb[dst],
                                      in1=sb["tka"], op=OP.add), tick=True)
    E("tensor", lambda e:
      e.matmul(psSW[0:1, 0:1], sb["sw_par"], sb["ones_col"],
               start=True, stop=True), waits=[P.t("vector")], tick=True)
    E("tensor", lambda e:
      e.matmul(psSW[0:1, 1:2], sb["sw_par2"], sb["ones_col"],
               start=True, stop=True), tick=True)
    E("scalar", lambda e:
      e.activation(out=sb["swv"], in_=ps[1][0:1, 169:171], func=AF.Copy),
      waits=[P.t("tensor")], tick=True)
    dmov("sync", cc_sw_in.ap(), sb["swv"], waits=[P.t("scalar")])
    E("gpsimd", lambda e:
      e.collective_compute("AllReduce", OP.add,
                           replica_groups=[list(range(NCORE))],
                           ins=[cc_sw_in.ap().opt()],
                           outs=[cc_sw_out.ap().opt()]),
      waits=[P.d("sync")], tick=True)
    dmov("sync", sb["swv"], cc_sw_out.ap(), waits=[P.t("gpsimd")])
    E("tensor", lambda e:
      e.matmul(ps[1][:, 169:171].opt(), sb["ones_row"][0:1, :], sb["swv"],
               start=True, stop=True), waits=[P.d("sync")], tick=True)
    E("scalar", lambda e:
      e.activation(out=sb["swk"], in_=ps[1][:, 169:171], func=AF.Copy),
      waits=[P.t("tensor")], tick=True)

    def seg_tokens(b, s):
        t0 = b * S + SEG * s
        return t0 // OWN, t0 % OWN

    w_gb = {}
    w_tap_done = {}
    w_copy_done = {}
    w_silu = {}
    w_xt_used = {}
    w_kqv_free = None
    w_ag_evac = None
    w_ntok_free = None
    xT_segs = [xT_seg, xT_segB]
    cvts = [(sb["cvt0"], sb["cvt1"]), (sb["cvt2"], sb["cvt3"])]
    for s in range(NSEG):
        spar = s % 2
        xTc = xT_segs[spar]
        lw = [w_agx, P.t("tensor")] if spar not in w_xt_used else \
             [w_xt_used[spar]]
        for b in range(2):
            oc, col = seg_tokens(b, s)
            dmov("sync", xTc[:, b, :, :],
                 cc_xt_out.ap()[oc, :, :, col:col + SEG].transpose([1, 0, 2]),
                 waits=(lw if b == 0 else []))
        w_xt = P.d("sync")
        first_silu = True
        for b in range(2):
            for ht in range(2):
                pbank = ps[2 * b + ht]
                projs = ((Wq_s, "cqw_s", "cqb_s"), (Wk_s, "ckw_s", "ckb_s"),
                         (Wv_s, "cvw_s", "cvb_s"))
                mw = [w_xt]
                if (b, ht) in w_copy_done:
                    mw.append(w_copy_done[(b, ht)])
                if s == 0:
                    mw += [w_ldg, P.t("scalar"), P.t("vector")]
                for ti, (Wsb, cwn, cbn) in enumerate(projs):
                    pslot = pbank[:, 128 * ti:128 * ti + 128]
                    for dd in range(16):
                        E("tensor", lambda e, Wsb=Wsb, dd=dd, ht=ht,
                          pslot=pslot, xTc=xTc, b=b, ti=ti:
                          e.matmul(pslot,
                                   Wsb[:, dd, 128 * ht:128 * ht + 128],
                                   xTc[:, b, dd, :],
                                   start=(ti == 0 and dd == 0),
                                   stop=(ti == 2 and dd == 15)),
                          waits=(mw if (ti == 0 and dd == 0) else ()),
                          tick=(dd == 15))
                wmm = P.t("tensor")
                for ti, (Wsb, cwn, cbn) in enumerate(projs):
                    pslot = pbank[:, 128 * ti:128 * ti + 128]
                    tix = (ti * 2 + b) * 2 + ht
                    cpar = tix % 2
                    cw = [wmm]
                    if tix in w_tap_done:
                        cw.append(w_tap_done[tix])
                    E("scalar", lambda e, pslot=pslot, tix=tix:
                      e.activation(out=uextt[:, tix, 3:3 + SEG], in_=pslot,
                                   func=AF.Copy), waits=cw, tick=True)
                    w_copy_done[(b, ht)] = P.t("scalar")
                    wcp = P.t("scalar")
                    E("vector", lambda e, tix=tix:
                      e.tensor_copy(out=uextt[:, tix, 0:3],
                                    in_=sb["tails"][:, 3 * tix:3 * tix
                                                    + 3]),
                      waits=[wcp], tick=True)
                    E("vector", lambda e, tix=tix:
                      e.tensor_copy(out=sb["tails"][:, 3 * tix:3 * tix + 3],
                                    in_=uextt[:, tix, SEG:SEG + 3]),
                      tick=True)
                    cv = cvts[cpar]
                    E("vector", lambda e, cwn=cwn, ht=ht, tix=tix, cv=cv:
                      e.tensor_scalar(out=cv[0], in0=uextt[:, tix, 0:SEG],
                                      scalar1=sb[cwn][:, ht, 0:1],
                                      scalar2=None, op0=OP.mult), tick=True)
                    for jj in (1, 2, 3):
                        jw = ([w_silu[cpar]]
                              if jj == 1 and cpar in w_silu else [])
                        E("vector", lambda e, cwn=cwn, ht=ht, jj=jj, cv=cv,
                          tix=tix:
                          e.scalar_tensor_tensor(
                              out=cv[jj % 2],
                              in0=uextt[:, tix, jj:jj + SEG],
                              scalar=sb[cwn][:, ht, jj:jj + 1],
                              in1=cv[(jj + 1) % 2], op0=OP.mult,
                              op1=OP.add),
                          waits=jw, tick=True)
                    wcv = P.t("vector")
                    w_tap_done[tix] = wcv
                    sw = [wcv]
                    if first_silu and w_kqv_free is not None:
                        sw.append(w_kqv_free)
                    first_silu = False
                    pp = 2 * b + ht
                    if ti < 2:
                        dstKQ = Qint_seg if ti == 0 else Kint_seg
                        E("scalar", lambda e, cbn=cbn, ht=ht, dstKQ=dstKQ,
                          pp=pp, cv=cv:
                          e.activation(out=dstKQ[:, :, :, pp].opt(),
                                       in_=cv[1], func=AF.Silu,
                                       bias=sb[cbn][:, ht:ht + 1]),
                          waits=sw, tick=True)
                    else:
                        E("scalar", lambda e, cbn=cbn, b=b, ht=ht, cv=cv:
                          e.activation(out=Vch_seg[:, b, ht, :],
                                       in_=cv[1], func=AF.Silu,
                                       bias=sb[cbn][:, ht:ht + 1]),
                          waits=sw, tick=True)
                    w_silu[cpar] = P.t("scalar")
        wkqv = P.t("scalar")
        wmm_ab = []
        for b in range(2):
            pag = psA if b == 0 else psB_
            pbt = psPK[:, 2 * b:2 * b + 2]
            mw = [w_xt]
            if w_ag_evac is not None:
                mw.append(w_ag_evac)
            if s == 0:
                mw += [P.t("vector"), P.t("scalar")]
            for dd in range(16):
                E("tensor", lambda e, b=b, dd=dd, pag=pag, xTc=xTc:
                  e.matmul(pag[:, 0:512], xTc[:, b, dd, :],
                           Wag_s[:, dd, :], start=(dd == 0), stop=False),
                  waits=(mw if dd == 0 else ()),
                  tick=False)
            E("tensor", lambda e, b=b, pag=pag:
              e.matmul(pag[:, 0:512], sb["onesr_bf"], sb["agb_bf"],
                       start=False, stop=True), tick=True)
            for dd in range(16):
                E("tensor", lambda e, b=b, dd=dd, pbt=pbt, xTc=xTc:
                  e.matmul(pbt, xTc[:, b, dd, :], Wb_s[:, dd, :],
                           start=(b == 0 and dd == 0), stop=False),
                  tick=False)
            E("tensor", lambda e, b=b, pbt=pbt:
              e.matmul(pbt, sb["onesr_bf"], sb["bb_bf"], start=False,
                       stop=(b == 1)), tick=True)
            wmm_ab.append(P.t("tensor"))
        w_xt_used[spar] = P.t("tensor")
        for b in range(2):
            pag = psA if b == 0 else psB_
            pbt = psPK[:, 2 * b:2 * b + 2]
            ew = [wmm_ab[b]]
            if b == 0 and w_ntok_free is not None:
                ew += list(w_ntok_free)
            E("scalar", lambda e, b=b, pag=pag:
              e.activation(out=nla_tok[:, b, :, :].opt(), in_=pag[:, 0:256],
                           func=AF.Sigmoid), waits=ew, tick=True)
            E("scalar", lambda e, b=b, pag=pag:
              e.activation(out=g_tok[:, b, :, :].opt(), in_=pag[:, 256:512],
                           func=AF.Sigmoid), tick=True)
            E("scalar", lambda e, b=b, s=s, pbt=pbt:
              e.activation(out=sb["nlb_all"][:, s, b, :], in_=pbt,
                           func=AF.Sigmoid),
              waits=([wmm_ab[1]] if b == 0 else []), tick=True)
        for b in range(2):
            E("scalar", lambda e, b=b:
              e.activation(out=nla_tok[:, b, :, :].opt(),
                           in_=nla_tok[:, b, :, :].opt(), func=AF.Ln),
              tick=True)
            E("scalar", lambda e, b=b, s=s:
              e.activation(out=sb["nlb_all"][:, s, b, :],
                           in_=sb["nlb_all"][:, s, b, :], func=AF.Ln),
              tick=True)
        w_ag_evac = P.t("scalar")
        w_evac = w_ag_evac
        dmov("sync", kint_sp.ap()[s],
             Kint_seg.rearrange("p a b c -> p (a b c)"), waits=[wkqv])
        dmov("sync", qint_sp.ap()[s],
             Qint_seg.rearrange("p a b c -> p (a b c)"), waits=[wkqv])
        dmov("sync", vch_sp.ap()[s],
             Vch_seg.rearrange("p a b c -> p (a b c)"), waits=[wkqv])
        w_kqv_free = P.d("sync")
        for cci in range(CPS):
            dmov("sync",
                 nlab_sp.ap()[s][:, 128 * cci:128 * cci + 128]
                 .rearrange("(b h t) v -> t b h v", b=2, h=2),
                 nla_tok[32 * cci:32 * cci + 32], waits=[w_evac])
            dmov("gpsimd",
                 gb_sp.ap()[s][:, 128 * cci:128 * cci + 128]
                 .rearrange("(b h t) v -> t b h v", b=2, h=2),
                 g_tok[32 * cci:32 * cci + 32],
                 waits=([w_evac] if cci == 0 else []))
        w_gb[s] = P.d("gpsimd")
        w_ntok_free = (P.d("sync"), P.d("gpsimd"))
    w_p1 = P.d("sync")


    dmov("gpsimd", wo_bf.ap(), dram["Wo"].ap())
    w_wocast = P.d("gpsimd")

    Vch_segs = [Vch_seg, Vch_segB]
    w_vch_ld = {}
    w_vch_used = {}
    dmov("sync", Vch_segs[0].rearrange("p a b c -> p (a b c)"),
         vch_sp.ap()[0], waits=[w_p1])
    w_vch_ld[0] = P.d("sync")
    PC = [None, None]
    CC = {}
    nlbf = sb["nlb_all"].rearrange("p a b c -> p (a b c)")

    def emit_prep(cg):
        par = cg % 2
        sx = str(par)
        s, cci = cg // CPS, cg % CPS
        v4 = cg % 4
        pc = {}
        PC[par] = pc
        Kb, Qb = sb["Kint_c" + sx], sb["Qint_c" + sx]
        Kc = lambda p: Kb.rearrange("p (b c) -> p b c", b=32)[:, :, p]
        Qc = lambda p: Qb.rearrange("p (b c) -> p b c", b=32)[:, :, p]
        kw = CC.get(("KQz", par), [w_p1, P.t("tensor")])
        dmov("sync", Kb, kint_sp.ap()[s][:, 128 * cci:128 * cci + 128],
             waits=kw)
        dmov("sync", Qb, qint_sp.ap()[s][:, 128 * cci:128 * cci + 128])
        dmov("sync", sb["nla_c" + sx],
             nlab_sp.ap()[s][:, 128 * cci:128 * cci + 128],
             waits=CC.get(("nla_used", par), [P.t("tensor")]))
        dmov("sync", sb["gbf_c" + sx],
             gb_sp.ap()[s][:, 128 * cci:128 * cci + 128],
             waits=CC.get(("gbf_used", par), [P.t("vector")]) + [w_gb[s]])
        pc["w_ld"] = P.d("sync")
        if cci == CPS - 1 and s + 1 < NSEG:
            vp = (s + 1) % 2
            dmov("sync",
                 Vch_segs[vp].rearrange("p a b c -> p (a b c)"),
                 vch_sp.ap()[s + 1],
                 waits=w_vch_used.get(vp, [P.t("tensor")]))
            w_vch_ld[vp] = P.d("sync")
        yield
        bw = CC.get("ps1_free", [P.t("scalar"), P.t("vector")])
        for p in range(4):
            E("tensor", lambda e, v4=v4, p=p, s=s:
              e.matmul(psBI, sb["selb"][:, 4 * v4 + p, :],
                       nlbf[:, 4 * s + p:4 * s + p + 1],
                       start=(p == 0), stop=(p == 3)),
              waits=(bw if p == 0 else ()), tick=(p == 3))
        wBI = P.t("tensor")
        yield
        E("scalar", lambda e, sx=sx:
          e.activation(out=sb["bpos_int" + sx], in_=psBI, func=AF.Exp),
          waits=[wBI] + CC.get(("par_v", par), []), tick=True)
        w_bpos_s = P.t("scalar")
        E("vector", lambda e, sx=sx:
          e.tensor_scalar(out=sb["btn_int" + sx], in0=sb["bpos_int" + sx],
                          scalar1=-1.0, scalar2=None, op0=OP.mult),
          waits=[w_bpos_s], tick=True)
        yield
        E("tensor", lambda e, Kb=Kb:
          e.matmul(psG, Kb, Kb, start=True, stop=True),
          waits=[pc["w_ld"], w_bpos_s], tick=True)
        yield
        for p in range(4):
            E("tensor", lambda e, p=p, Kc=Kc, Qc=Qc:
              e.matmul(psGq[32 * p:32 * p + 32, :], Kc(p), Qc(p),
                       start=True, stop=True, tile_position=(0, 32 * p)),
              tick=(p == 3))
        wG = P.t("tensor")
        yield
        E("vector", lambda e, sx=sx:
          e.tensor_tensor(out=sb["Gblk" + sx], in0=psG, in1=sb["gkk_mask"],
                          op=OP.mult),
          waits=[wG] + CC.get(("par_t", par), []), tick=True)
        pc["w_gblk"] = P.t("vector")
        E("vector", lambda e, sx=sx:
          e.tensor_tensor(out=sb["GqT" + sx], in0=psGq, in1=sb["gq_mask"],
                          op=OP.mult), tick=True)
        pc["w_gqt"] = P.t("vector")
        wGe = pc["w_gqt"]
        yield
        cw = [pc["w_ld"]] + CC.get("exps_s", [])
        for i, (dst, M) in enumerate(((psL1, "M1c"), (psL5, "M5"),
                                      (psL4, "M4"), (psNI, "PERMib"))):
            E("tensor", lambda e, dst=dst, M=M, sx=sx:
              e.matmul(dst, sb[M], sb["nla_c" + sx], start=True, stop=True),
              waits=(cw if i == 0 else ()), tick=(i == 3))
        yield
        E("tensor", lambda e, sx=sx:
          e.matmul(psNC, sb["nla_c" + sx], sb["indC"], start=True,
                   stop=True), waits=[wGe], tick=True)
        E("tensor", lambda e, sx=sx:
          e.matmul(psNE, sb["nla_c" + sx], sb["indE"], start=True,
                   stop=True), tick=True)
        wLA = P.t("tensor")
        CC[("nla_used", par)] = [wLA]
        yield
        first = [wLA] + CC.get(("par_v", par), [])
        for i, (dst, psrc, kw2) in enumerate((
                ("E2_blk", psL1, dict(scale=-1.0)),
                ("AC_blk", psL1, {}),
                ("E1_blk", psL5, "LNS"),
                ("Af_blk", psL4, {}),
                ("Ainv_int", psNI, dict(scale=-1.0)),
                ("ACf_c", psNC, {}),
                ("ACe_c", psNE, {}))):
            if kw2 == "LNS":
                E("scalar", lambda e, dst=dst, psrc=psrc, sx=sx:
                  e.activation(out=sb[dst + sx], in_=psrc, func=AF.Exp,
                               scale=-1.0, bias=sb["lns_col"][:, 0:1]),
                  tick=True)
            else:
                E("scalar", lambda e, dst=dst, psrc=psrc, kw2=kw2, sx=sx:
                  e.activation(out=sb[dst + sx], in_=psrc, func=AF.Exp,
                               **kw2),
                  waits=(first if i == 0 else ()), tick=True)
        pc["w_exps"] = P.t("scalar")
        CC["exps_s"] = [pc["w_exps"]]
        yield
        vp = s % 2
        tw = (CC.get("t2v", [P.t("vector")])
              + CC.get("ktok_s", [P.t("scalar")]) + [w_vch_ld[vp]])
        for p in range(4):
            E("tensor", lambda e, p=p, cci=cci, vp=vp:
              e.transpose(psVtb[32 * p:32 * p + 32, :],
                          Vch_segs[vp][:, p // 2, p % 2,
                                       32 * cci:32 * cci + 32],
                          sb["ident_bf"], tile_position=(0, 32 * p)),
              waits=(tw if p == 0 else ()), tick=(p == 3))
        w_vch_used[vp] = [P.t("tensor")]
        yield
        for p in range(4):
            E("tensor", lambda e, p=p, Kc=Kc:
              e.transpose(psKtb[32 * p:32 * p + 32, :], Kc(p),
                          sb["ident_bf"], tile_position=(0, 32 * p)),
              tick=(p == 3))
        pc["w_vt"] = P.t("tensor")
        yield
        E("scalar", lambda e, sx=sx:
          e.activation(out=sb["Ktok_blk" + sx], in_=psKtb, func=AF.Copy,
                       scale=SC), waits=[pc["w_vt"]]
          + CC.get(("par_t", par), []), tick=True)
        pc["w_ktok"] = P.t("scalar")
        CC["ktok_s"] = [pc["w_ktok"]]
        CC["ps1_free"] = [pc["w_exps"], pc["w_gqt"]]
        yield

    gen0 = emit_prep(0)
    for _ in gen0:
        pass
    gnext = None
    for cg in range(NCH if not debug else DBG_CHUNK + 1):
        par = cg % 2
        sx = str(par)
        pc = PC[par]
        Kb, Qb = sb["Kint_c" + sx], sb["Qint_c" + sx]
        Kc = lambda p, Kb=Kb: Kb.rearrange("p (b c) -> p b c", b=32)[:, :, p]
        Qc = lambda p, Qb=Qb: Qb.rearrange("p (b c) -> p b c", b=32)[:, :, p]
        if not debug and cg + 1 < NCH:
            gnext = emit_prep(cg + 1)
        else:
            gnext = None
        rw = ([pc["w_ld"]] + CC.get("S0T_s", [P.t("scalar")])
              + CC.get("oev", [P.t("vector")]))
        for p in range(4):
            E("tensor", lambda e, p=p, Kc=Kc:
              e.matmul(psR0[32 * p:32 * p + 32, :], Kc(p), sb[f"S0T{p}"],
                       start=True, stop=True, tile_position=(0, 32 * p)),
              waits=(rw if p == 0 else ()), tick=(p == 3))
            E("tensor", lambda e, p=p, Qc=Qc:
              e.matmul(psQ0[32 * p:32 * p + 32, :], Qc(p), sb[f"S0T{p}"],
                       start=True, stop=True, tile_position=(0, 32 * p)),
              tick=(p == 3))
        wR0 = P.t("tensor")
        CC[("KQz", par)] = [wR0]
        E("vector", lambda e, sx=sx:
          e.scalar_tensor_tensor(out=sb["t1_blk"], in0=psR0, scalar=-1.0,
                                 in1=sb["E1_blk" + sx], op0=OP.mult,
                                 op1=OP.mult),
          waits=[wR0, pc["w_exps"]], tick=True)
        E("vector", lambda e, sx=sx:
          e.scalar_tensor_tensor(out=sb["t2_blk"], in0=psVtb, scalar=-1.0,
                                 in1=sb["E2_blk" + sx], op0=OP.mult,
                                 op1=OP.mult),
          waits=[pc["w_vt"]], tick=True)
        CC["t2v"] = [P.t("vector")]
        E("vector", lambda e:
          e.tensor_tensor(out=sb["Rt_blk"], in0=sb["t1_blk"],
                          in1=sb["t2_blk"], op=OP.add), tick=True)
        wRt = P.t("vector")
        E("tensor", lambda e:
          e.matmul(psRI, sb["PERMib"], sb["Rt_blk"], start=True, stop=True),
          waits=[wRt], tick=True)
        wRImm = P.t("tensor")
        E("scalar", lambda e, sx=sx:
          e.activation(out=sb["Rt_int"], in_=psRI, func=AF.Copy,
                       scale=sb["bpos_int" + sx][:, 0:1]),
          waits=[wRImm], tick=True)
        wRi = P.t("scalar")

        wstep_v = P.t("vector")
        E("tensor", lambda e, sx=sx:
          e.matmul(psW, sb["Gblk" + sx], sb["Rt_int"], start=True,
                   stop=True),
          waits=[wRi, pc["w_gblk"], wstep_v] + CC.get("qz", []), tick=True)
        for t in range(C):
            E("tensor", lambda e, sx=sx, t=t:
              e.matmul(psW, sb["Gblk" + sx], sb["Q_int"], start=False,
                       stop=True, skip_group_check=True),
              waits=([] if t == 0 else [wstep_v]), tick=True)
            wmm = P.t("tensor")
            E("vector", lambda e, sx=sx:
              e.scalar_tensor_tensor(out=sb["Q_int"], in0=psW,
                                     scalar=sb["btn_int" + sx][:, 0:1],
                                     in1=sb["Ainv_int" + sx], op0=OP.mult,
                                     op1=OP.mult),
              waits=([wmm, pc["w_exps"]] if t == 0 else [wmm]), tick=True)
            wstep_v = P.t("vector")
            if gnext is not None:
                next(gnext, None)
        if gnext is not None:
            for _ in gnext:
                pass

        def pnext():
            pass

        E("tensor", lambda e:
          e.matmul(psPB, sb["permbi_bf"], sb["Rt_int"], start=True,
                   stop=False), waits=[wstep_v], tick=False)
        E("tensor", lambda e:
          e.matmul(psPB, sb["permbi_bf"], sb["Q_int"], start=False,
                   stop=True), tick=True)
        pnext()
        wPBmm = P.t("tensor")
        E("scalar", lambda e:
          e.activation(out=sb["P_blk"], in_=psPB, func=AF.Copy),
          waits=[wPBmm], tick=True)
        wPb = P.t("scalar")
        E("vector", lambda e:
          e.memset(sb["Q_int"], 0.0), waits=[wPBmm], tick=True)
        CC["qz"] = [P.t("vector")]
        pnext()
        for p in range(4):
            E("tensor", lambda e, p=p, sx=sx:
              e.matmul(psGP[32 * p:32 * p + 32, :],
                       sb["GqT" + sx][32 * p:32 * p + 32, :],
                       sb["P_blk"][32 * p:32 * p + 32, :],
                       start=True, stop=True,
                       tile_position=(32 * p, 32 * p)),
              waits=([wPb, pc["w_gqt"]] if p == 0 else ()), tick=(p == 3))
            E("tensor", lambda e, p=p, sx=sx:
              e.matmul(psPK[:, 128 * p:128 * p + 128],
                       sb["P_blk"][32 * p:32 * p + 32, :],
                       sb["Ktok_blk" + sx][32 * p:32 * p + 32, :],
                       start=True, stop=True, tile_position=(32 * p, 0)),
              waits=([pc["w_ktok"]] if p == 0 else ()), tick=(p == 3))
        wOP = P.t("tensor")
        pnext()
        E("vector", lambda e, sx=sx:
          e.scalar_tensor_tensor(out=sb["t1_blk"], in0=psQ0, scalar=-1.0,
                                 in1=sb["Af_blk" + sx], op0=OP.mult,
                                 op1=OP.mult),
          waits=[wOP], tick=True)
        CC["oev"] = [P.t("vector")]
        E("vector", lambda e, sx=sx:
          e.scalar_tensor_tensor(out=sb["t2_blk"], in0=psGP, scalar=-1.0,
                                 in1=sb["AC_blk" + sx], op0=OP.mult,
                                 op1=OP.mult), tick=True)
        E("vector", lambda e:
          e.tensor_tensor(out=sb["o_blk"], in0=sb["t1_blk"],
                          in1=sb["t2_blk"], op=OP.add), tick=True)
        E("vector", lambda e:
          e.tensor_reduce(sb["mu_c"], sb["o_blk"], AX.X, OP.add), tick=True)
        wLN = P.t("vector")
        pnext()
        E("scalar", lambda e:
          e.activation(out=sb["t1_blk"], in_=sb["o_blk"], func=AF.Square,
                       accum_out=sb["sq_c"]), waits=[wLN], tick=True)
        wSQ = P.t("scalar")
        E("vector", lambda e:
          e.tensor_scalar(out=sb["mu_c"], in0=sb["mu_c"], scalar1=1.0 / 128,
                          scalar2=None, op0=OP.mult), waits=[wSQ], tick=True)
        E("vector", lambda e:
          e.scalar_tensor_tensor(out=sb["va_c"], in0=sb["mu_c"],
                                 scalar=-1.0, in1=sb["mu_c"], op0=OP.mult,
                                 op1=OP.mult), tick=True)
        E("vector", lambda e:
          e.scalar_tensor_tensor(out=sb["va_c"], in0=sb["sq_c"],
                                 scalar=1.0 / 128, in1=sb["va_c"],
                                 op0=OP.mult, op1=OP.add), tick=True)
        E("vector", lambda e:
          e.tensor_scalar(out=sb["va_c"], in0=sb["va_c"], scalar1=1e-5,
                          scalar2=None, op0=OP.add), tick=True)
        wVA = P.t("vector")
        pnext()
        E("scalar", lambda e:
          e.activation(out=sb["rstd_c"], in_=sb["va_c"], func=AF.Ln),
          waits=[wVA], tick=True)
        E("scalar", lambda e:
          e.activation(out=sb["rstd_c"], in_=sb["rstd_c"], func=AF.Exp,
                       scale=-0.5), tick=True)
        wRS = P.t("scalar")
        E("vector", lambda e:
          e.scalar_tensor_tensor(out=sb["t1_blk"], in0=sb["o_blk"],
                                 scalar=sb["mu_c"], in1=sb["lnw_s"],
                                 op0=OP.subtract, op1=OP.mult),
          waits=[wRS], tick=True)
        E("vector", lambda e:
          e.tensor_scalar(out=sb["t1_blk"], in0=sb["t1_blk"],
                          scalar1=sb["rstd_c"], scalar2=None, op0=OP.mult),
          tick=True)
        E("vector", lambda e:
          e.tensor_tensor(out=sb["t2_blk"], in0=sb["t1_blk"],
                          in1=sb["lnb_s"], op=OP.add), tick=True)
        E("vector", lambda e, sx=sx:
          e.tensor_tensor(out=sb["og_blk"], in0=sb["t2_blk"],
                          in1=sb["gbf_c" + sx], op=OP.mult),
          waits=CC.get("og_dma", []), tick=True)
        wOG = P.t("vector")
        pnext()
        CC[("gbf_used", par)] = [wOG]
        dmov("sync", cc_og_in.ap()[:, C * cg:C * cg + C, :],
             sb["og_blk"], waits=[wOG])
        CC["og_dma"] = [P.d("sync")]
        for p in range(4):
            E("scalar", lambda e, p=p, sx=sx:
              e.activation(out=sb["t1_blk"] if p % 2 == 0 else sb["t2_blk"],
                           in_=sb[f"Sneg{p}"], func=AF.Copy,
                           scale=sb["ACf_c" + sx][:, p:p + 1]),
              waits=([wOP, P.t("vector")] if p == 0 else [P.t("vector")]),
              tick=True)
            wS1 = P.t("scalar")
            E("vector", lambda e, p=p, sx=sx:
              e.scalar_tensor_tensor(out=sb[f"Sneg{p}"],
                                     in0=psPK[:, 128 * p:128 * p + 128],
                                     scalar=sb["ACe_c" + sx][:, p:p + 1],
                                     in1=sb["t1_blk"] if p % 2 == 0
                                     else sb["t2_blk"], op0=OP.mult,
                                     op1=OP.add), waits=[wS1], tick=True)
            wSU = P.t("vector")
            E("tensor", lambda e, p=p:
              e.transpose(psST, sb[f"Sneg{p}"], sb["ident"]),
              waits=[wSU, P.t("scalar")], tick=True)
            E("scalar", lambda e, p=p:
              e.activation(out=sb[f"S0T{p}"], in_=psST, func=AF.Copy),
              waits=[P.t("tensor")], tick=True)
            pnext()
        if gnext is not None:
            for _ in gnext:
                pass
        CC["S0T_s"] = [P.t("scalar")]
        CC[("par_v", par)] = [P.t("vector")]
        CC[("par_t", par)] = [P.t("tensor")]
        if debug and cg == DBG_CHUNK:
            wall = [P.t("vector"), P.t("scalar"), P.t("tensor")]
            for k in dbg_names:
                k2 = k + "0" if (k + "0") in sb else k
                src_ = sb[k2]
                if src_.dtype != F32:
                    E("vector", lambda e, src_=src_:
                      e.tensor_copy(out=sb["t2_blk"][:, 0:src_.shape[1]],
                                    in_=src_),
                      waits=list(wall) + [P.d("gpsimd")])
                    E("gpsimd", lambda e, k=k, src_=src_: e.dma_start(
                        out=dbg_t[k].ap()[:, 0:src_.shape[1]],
                        in_=sb["t2_blk"][:, 0:src_.shape[1]],
                        allow_slow_non_contiguous=True),
                      waits=[P.t("vector")], dma=True)
                else:
                    E("gpsimd", lambda e, k=k, src_=src_: e.dma_start(
                        out=dbg_t[k].ap()[:, 0:src_.shape[1]], in_=src_,
                        allow_slow_non_contiguous=True),
                      waits=wall, dma=True)
    if debug:
        E("vector", lambda e: e.engine_nop(),
          waits=[P.d("gpsimd"), P.d("sync")])
        return _finish(nc, P, sems_holder=None)
    w_scan_s = P.d("sync")
    E("gpsimd", lambda e:
      e.collective_compute("AllGather", OP.bypass,
                           replica_groups=[list(range(NCORE))],
                           ins=[cc_og_in.ap().opt()],
                           outs=[cc_og_out.ap()
                                 .rearrange("(n a) c -> n a c", n=NCORE)
                                 .opt()]),
      waits=[w_scan_s, P.t("vector"), P.t("scalar")], tick=True)
    w_agog = P.t("gpsimd")

    ogi = sb["ogi_s"]

    def ogidx(tc, src, ht):
        k = (tc * NCORE + src) * 2 + ht
        return ogi[:, k:k + 1]

    Wo_r = wo_bf.ap().rearrange("(n p) c -> p n c", p=128)
    w1T_r = dram["w1T"].ap().rearrange("(n p) c -> p n c", p=128)
    w2T_r = dram["w2T"].ap().rearrange("(n p) c -> n p c", p=128)
    x_own_rr = dram["x_own"].ap().rearrange("(n p) d -> n p d", p=128)

    og_owns = [og_own, og_ownB]
    ogTs = [ogT, ogTB]
    Wo_bands = [Wo_band, Wo_bandB]
    x1ts = [x1t, ytmpB]
    xrs = [xr, xrB]
    w_psbf = {0: None, 1: None}
    w_tpose = None
    w_ogT_used = None
    w_mmWo = {}
    w_bankA = {}
    w_xr_free = None
    w_xqT_used = None
    w_x1_free = {}
    w_boot = [P.t("vector"), P.t("scalar")]
    for tcp in range(NTC // 2):
        pair = (2 * tcp, 2 * tcp + 1)
        gw0 = [w_agog] + ([w_tpose] if w_tpose is not None else [])
        wogs = []
        for ti, tc in enumerate(pair):
            for src in range(NCORE):
                for ht in range(2):
                    E("gpsimd", lambda e, tc=tc, src=src, ht=ht, ti=ti:
                      e.indirect_dma_start(
                          og_owns[ti][:, 256 * src + 128 * ht:
                                      256 * src + 128 * ht + 128],
                          None, cc_og_out.ap(),
                          IndirectOffsetOnAxis(ap=ogidx(tc, src, ht), axis=0)),
                      waits=(gw0 if (ti, src, ht) == (0, 0, 0) else []),
                      dma=True)
            wogs.append(P.d("gpsimd"))
        for ti in range(2):
            for dd in range(16):
                tw = [wogs[ti]]
                if w_psbf[dd % 2] is not None:
                    tw.append(w_psbf[dd % 2])
                if tcp == 0 and ti == 0 and dd < 2:
                    tw += w_boot
                E("tensor", lambda e, dd=dd, ti=ti:
                  e.transpose(psbf[dd % 2],
                              og_owns[ti][:, 128 * dd:128 * (dd + 1)],
                              sb["ident_bf"]), waits=tw, tick=True)
                cw = [P.t("tensor")]
                if ti == 0 and dd == 0 and w_ogT_used is not None:
                    cw.append(w_ogT_used)
                E("scalar", lambda e, dd=dd, ti=ti:
                  e.activation(out=ogTs[ti][:, dd, :], in_=psbf[dd % 2],
                               func=AF.Copy), waits=cw, tick=True)
                w_psbf[dd % 2] = P.t("scalar")
        w_tpose = P.t("tensor")
        wogT = P.t("scalar")
        xw = list(w_xr_free) if w_xr_free is not None else []
        dmov("sync", xrs[0], x_own_rr[pair[0]], waits=xw)
        dmov("sync", xrs[1], x_own_rr[pair[1]])
        wxr = P.d("sync")
        for band in range(4):
            bp = band % 2
            lw = [w_mmWo[bp]] if bp in w_mmWo else [P.t("tensor"),
                                                    w_wocast]
            dmov("sync", Wo_bands[bp],
                 Wo_r[:, :, 512 * band:512 * (band + 1)], waits=lw)
            wWo = P.d("sync")
            for ti in range(2):
                pband = psA if ti == 0 else psB_
                mw = [wWo, wogT]
                if ti in w_bankA:
                    mw.append(w_bankA[ti])
                if tcp == 0 and band == 0:
                    mw = mw + w_boot
                for chn in range(16):
                    E("tensor", lambda e, chn=chn, ti=ti, bp=bp, pband=pband:
                      e.matmul(pband, ogTs[ti][:, chn, :],
                               Wo_bands[bp][:, chn, :],
                               start=(chn == 0), stop=(chn == 15)),
                      waits=(mw if chn == 0 else ()), tick=(chn == 15))
                wA = P.t("tensor")
                ew = [wA, wxr]
                if band == 0 and ti in w_x1_free:
                    ew += list(w_x1_free[ti])
                E("vector", lambda e, band=band, ti=ti, pband=pband:
                  e.tensor_tensor(out=x1ts[ti][:, 512 * band:
                                               512 * (band + 1)],
                                  in0=pband,
                                  in1=xrs[ti][:, 512 * band:512 * (band + 1)],
                                  op=OP.add), waits=ew, tick=True)
                w_bankA[ti] = P.t("vector")
            w_mmWo[bp] = P.t("tensor")
        w_ogT_used = P.t("tensor")
        wx1 = P.t("vector")
        for ti, tc in enumerate(pair):
            x1c = x1ts[ti]
            scr = xrs[ti]
            E("vector", lambda e, x1c=x1c:
              e.tensor_reduce(sb["mu_c"], x1c, AX.X, OP.add),
              waits=[wx1], tick=True)
            E("scalar", lambda e, x1c=x1c, scr=scr:
              e.activation(out=scr, in_=x1c, func=AF.Square,
                           accum_out=sb["sq_c"]),
              waits=[wx1, P.t("vector")], tick=True)
            wst = P.t("scalar")
            E("vector", lambda e:
              e.tensor_scalar(out=sb["mu_c"], in0=sb["mu_c"],
                              scalar1=1.0 / HID, scalar2=None, op0=OP.mult),
              waits=[wst], tick=True)
            E("vector", lambda e:
              e.scalar_tensor_tensor(out=sb["va_c"], in0=sb["mu_c"],
                                     scalar=-1.0, in1=sb["mu_c"],
                                     op0=OP.mult, op1=OP.mult), tick=True)
            E("vector", lambda e:
              e.scalar_tensor_tensor(out=sb["va_c"], in0=sb["sq_c"],
                                     scalar=1.0 / HID, in1=sb["va_c"],
                                     op0=OP.mult, op1=OP.add), tick=True)
            E("vector", lambda e:
              e.tensor_scalar(out=sb["va_c"], in0=sb["va_c"], scalar1=1e-5,
                              scalar2=None, op0=OP.add), tick=True)
            wva = P.t("vector")
            E("scalar", lambda e:
              e.activation(out=sb["rstd_c"], in_=sb["va_c"], func=AF.Ln),
              waits=[wva], tick=True)
            E("scalar", lambda e:
              e.activation(out=sb["rstd_c"], in_=sb["rstd_c"], func=AF.Exp,
                           scale=-0.5), tick=True)
            wrs = P.t("scalar")
            E("vector", lambda e, x1c=x1c:
              e.tensor_scalar(out=x1c, in0=x1c, scalar1=sb["mu_c"],
                              scalar2=sb["rstd_c"], op0=OP.subtract,
                              op1=OP.mult), waits=[wrs], tick=True)
            wx1n = P.t("vector")
            dmov("sync", x1_sp.ap()[tc], x1c, waits=[wx1n])
            E("scalar", lambda e, x1c=x1c, scr=scr:
              e.activation(out=scr, in_=x1c, func=AF.Square,
                           accum_out=sb["sq_c"]), waits=[wx1n], tick=True)
            E("vector", lambda e, x1c=x1c:
              e.tensor_reduce(sb["habs"], x1c, AX.X, OP.max,
                              apply_absolute_value=True), waits=[wx1n],
              tick=True)
            E("vector", lambda e:
              e.tensor_scalar(out=sb["tka"], in0=sb["sq_c"],
                              scalar1=1.0 / HID, scalar2=1e-6, op0=OP.mult,
                              op1=OP.add), waits=[P.t("scalar")], tick=True)
            wtk = P.t("vector")
            E("scalar", lambda e:
              e.activation(out=sb["tka"], in_=sb["tka"], func=AF.Ln),
              waits=[wtk], tick=True)
            E("scalar", lambda e:
              e.activation(out=sb["tka"], in_=sb["tka"], func=AF.Exp,
                           scale=-0.5), tick=True)
            wk1 = P.t("scalar")
            E("vector", lambda e:
              e.tensor_scalar(out=sb["tkb"], in0=sb["habs"],
                              scalar1=sb["tka"], scalar2=ISQH, op0=OP.mult,
                              op1=OP.mult), waits=[wk1], tick=True)
            E("vector", lambda e:
              e.tensor_scalar(out=sb["tkb"], in0=sb["tkb"], scalar1=1e-5,
                              scalar2=None, op0=OP.max), tick=True)
            E("vector", lambda e, tc=tc:
              e.tensor_scalar(out=sb["ysc1_all"][:, tc:tc + 1],
                              in0=sb["tkb"], scalar1=sb["swk"][:, 0:1],
                              scalar2=1.0 / (127.0 * HID * INNER),
                              op0=OP.mult, op1=OP.mult), tick=True)
            wk2 = P.t("vector")
            E("scalar", lambda e:
              e.activation(out=sb["tkb"], in_=sb["tkb"], func=AF.Ln),
              waits=[wk2], tick=True)
            E("scalar", lambda e:
              e.activation(out=sb["tkb"], in_=sb["tkb"], func=AF.Exp,
                           scale=-1.0), tick=True)
            wk3 = P.t("scalar")
            E("vector", lambda e:
              e.tensor_scalar(out=sb["kq_c"], in0=sb["tkb"],
                              scalar1=sb["tka"], scalar2=127.0 * ISQH,
                              op0=OP.mult, op1=OP.mult),
              waits=[wk3], tick=True)
            E("vector", lambda e, x1c=x1c, scr=scr:
              e.tensor_scalar(out=scr, in0=x1c, scalar1=sb["kq_c"],
                              scalar2=None, op0=OP.mult), tick=True)
            E("vector", lambda e, scr=scr:
              e.tensor_scalar(out=scr, in0=scr, scalar1=RND, scalar2=-RND,
                              op0=OP.add, op1=OP.add), tick=True)
            E("vector", lambda e, scr=scr:
              e.tensor_scalar(out=scr, in0=scr, scalar1=127.0,
                              scalar2=-128.0, op0=OP.min, op1=OP.max),
              tick=True)
            aw = [w_xqT_used] if w_xqT_used is not None else []
            E("vector", lambda e, tc=tc, scr=scr:
              e.tensor_scalar(out=xq_bf, in0=scr, scalar1=0.0, scalar2=0.0,
                              op0=OP.add, op1=OP.add,
                              accum_out=sb["zsum"][:, tc:tc + 1]),
              waits=aw, tick=True)
            wxq = P.t("vector")
            for dd in range(16):
                tw = [wxq]
                if w_psbf[dd % 2] is not None:
                    tw.append(w_psbf[dd % 2])
                E("tensor", lambda e, dd=dd:
                  e.transpose(psbf[dd % 2],
                              xq_bf[:, 128 * dd:128 * (dd + 1)],
                              sb["ident_bf"]), waits=tw, tick=True)
                E("scalar", lambda e, dd=dd, tc=tc:
                  e.activation(out=xqT_all[:, dd, tc, :],
                               in_=psbf[dd % 2], func=AF.Copy),
                  waits=[P.t("tensor")], tick=True)
                w_psbf[dd % 2] = P.t("scalar")
            w_xqT_used = P.t("tensor")
            w_x1_free[ti] = (P.t("vector"), P.t("scalar"), P.d("sync"))
        w_xr_free = (P.t("vector"), P.t("scalar"))

    wldbs = [wldb, wldbB]
    w_ts_done = {}
    w_sp_done = {}
    w_a2boot = [P.t("vector"), P.t("tensor"), P.d("sync")]
    for k in range(64):
        par = k % 2
        lw = ([w_sp_done[par]] if par in w_sp_done else list(w_a2boot))
        dmov("gpsimd", wldbs[par][:, 0:HID], w2T_r[k], waits=lw)
        wld = P.d("gpsimd")
        E("vector", lambda e, par=par:
          e.tensor_scalar(out=wldbs[par][:, 0:HID],
                          in0=wldbs[par][:, 0:HID],
                          scalar1=0.0, scalar2=2.0, op0=OP.is_gt,
                          op1=OP.mult), waits=[wld], tick=True)
        w_ts_done[par] = P.t("vector")
        dmov("sync", w2s_sp.ap()[k], wldbs[par][:, 0:HID],
             waits=[w_ts_done[par]])
        w_sp_done[par] = P.d("sync")

    dmov("gpsimd", b1rep_bf, dram["b1_rep"].ap(),
         waits=[P.t("vector"), P.t("scalar")])
    dmov("gpsimd", b2rep_bf, dram["b2_rep"].ap())

    w1bufs = [w1s, w1sB]
    gts = [hq_t[:, 0:512], hgel[:, 0:512]]
    htfs = [htf, htf2]
    w_mmw1 = {}
    w_bankB1 = {}
    w_gelu = {}
    w_rd = {}
    w_g2sp = {}
    for it in range(16):
        wpar = it % 2
        lw = [w_mmw1[wpar]] if wpar in w_mmw1 else [P.t("tensor")]
        dmov("gpsimd", w1bufs[wpar], w1T_r[:, :, 512 * it:512 * (it + 1)],
             waits=lw)
        wld = P.d("gpsimd")
        E("vector", lambda e, wpar=wpar:
          e.tensor_scalar(out=w1bufs[wpar].rearrange("p a b -> p (a b)"),
                          in0=w1bufs[wpar].rearrange("p a b -> p (a b)"),
                          scalar1=0.0, scalar2=2.0, op0=OP.is_gt,
                          op1=OP.mult), waits=[wld], tick=True)
        wsg = P.t("vector")
        for tc in range(NTC):
            par = tc % 2
            pslot = psA if par == 0 else psB_
            mw = [wsg]
            if par in w_bankB1:
                mw.append(w_bankB1[par])
            if it == 0 and tc < 2:
                mw += [P.t("vector"), P.t("scalar")]
            for dd in range(16):
                E("tensor", lambda e, dd=dd, tc=tc, wpar=wpar, pslot=pslot:
                  e.matmul(pslot, xqT_all[:, dd, tc, :],
                           w1bufs[wpar][:, dd, :],
                           start=(dd == 0), stop=(dd == 15)),
                  waits=(mw if dd == 0 else ()), tick=(dd == 15))
            wh = P.t("tensor")
            hw = [wh]
            if par in w_gelu:
                hw.append(w_gelu[par])
            E("vector", lambda e, tc=tc, pslot=pslot, par=par:
              e.tensor_scalar(out=htfs[par], in0=pslot,
                              scalar1=sb["zsum"][:, tc:tc + 1],
                              scalar2=sb["ysc1_all"][:, tc:tc + 1],
                              op0=OP.subtract, op1=OP.mult),
              waits=hw, tick=True)
            w_bankB1[par] = P.t("vector")
            E("vector", lambda e, it=it, par=par:
              e.tensor_tensor(out=htfs[par], in0=htfs[par],
                              in1=b1rep_bf[:, 512 * it:512 * (it + 1)],
                              op=OP.add), tick=True)
            whf = P.t("vector")
            gw = [whf]
            if par in w_g2sp:
                gw += [w_g2sp[par], w_rd[par]]
            E("scalar", lambda e, par=par:
              e.activation(out=gts[par], in_=htfs[par], func=AF.Gelu),
              waits=gw, tick=True)
            w_gelu[par] = P.t("scalar")
            tkan = "tka" if par == 0 else "tka1"
            tkbn = "tkb" if par == 0 else "tkb1"
            E("vector", lambda e, par=par, tkan=tkan:
              e.tensor_reduce(sb[tkan], gts[par], AX.X, OP.max,
                              apply_absolute_value=True),
              waits=[P.t("scalar")], tick=True)
            w_rd[par] = P.t("vector")
            if it == 0:
                E("vector", lambda e, tc=tc, tkan=tkan:
                  e.tensor_copy(out=sb["habs8"][:, tc:tc + 1],
                                in_=sb[tkan]), tick=True)
            else:
                E("vector", lambda e, tc=tc, tkan=tkan:
                  e.tensor_tensor(out=sb["habs8"][:, tc:tc + 1],
                                  in0=sb["habs8"][:, tc:tc + 1],
                                  in1=sb[tkan], op=OP.max), tick=True)
            wgt = P.t("vector")
            E("scalar", lambda e, par=par, tkbn=tkbn:
              e.activation(out=ytmp[:, 0:512] if par == 0
                           else ytmp[:, 512:1024], in_=gts[par],
                           func=AF.Square, accum_out=sb[tkbn]),
              waits=[wgt], tick=True)
            if it == 0:
                E("vector", lambda e, tc=tc, tkbn=tkbn:
                  e.tensor_copy(out=sb["hssq8"][:, tc:tc + 1],
                                in_=sb[tkbn]), waits=[P.t("scalar")],
                  tick=True)
            else:
                E("vector", lambda e, tc=tc, tkbn=tkbn:
                  e.tensor_tensor(out=sb["hssq8"][:, tc:tc + 1],
                                  in0=sb["hssq8"][:, tc:tc + 1],
                                  in1=sb[tkbn], op=OP.add),
                  waits=[P.t("scalar")], tick=True)
            dmov("sync", g2_sp.ap()[tc, it], gts[par],
                 waits=[P.t("vector")])
            w_g2sp[par] = P.d("sync")
        w_mmw1[wpar] = P.t("tensor")
    w_b1 = P.d("sync")

    hstgs = [hstg, hstgB]
    w_tposeB2 = {}
    w_hstg_sp = {}
    for tc in range(NTC):
        E("vector", lambda e, tc=tc:
          e.tensor_scalar(out=sb["tka"], in0=sb["hssq8"][:, tc:tc + 1],
                          scalar1=1.0 / INNER, scalar2=1e-6, op0=OP.mult,
                          op1=OP.add), waits=[w_b1], tick=True)
        wtk = P.t("vector")
        E("scalar", lambda e:
          e.activation(out=sb["tka"], in_=sb["tka"], func=AF.Ln),
          waits=[wtk], tick=True)
        E("scalar", lambda e:
          e.activation(out=sb["tka"], in_=sb["tka"], func=AF.Exp,
                       scale=-0.5), tick=True)
        wk1 = P.t("scalar")
        E("vector", lambda e, tc=tc:
          e.tensor_scalar(out=sb["tkb"], in0=sb["habs8"][:, tc:tc + 1],
                          scalar1=sb["tka"], scalar2=ISQI, op0=OP.mult,
                          op1=OP.mult), waits=[wk1], tick=True)
        E("vector", lambda e:
          e.tensor_scalar(out=sb["tkb"], in0=sb["tkb"], scalar1=1e-5,
                          scalar2=None, op0=OP.max), tick=True)
        E("vector", lambda e, tc=tc:
          e.tensor_scalar(out=sb["ysc2_all"][:, tc:tc + 1], in0=sb["tkb"],
                          scalar1=sb["swk"][:, 1:2],
                          scalar2=1.0 / (127.0 * INNER * HID), op0=OP.mult,
                          op1=OP.mult), tick=True)
        wk2 = P.t("vector")
        E("scalar", lambda e:
          e.activation(out=sb["tkb"], in_=sb["tkb"], func=AF.Ln),
          waits=[wk2], tick=True)
        E("scalar", lambda e:
          e.activation(out=sb["tkb"], in_=sb["tkb"], func=AF.Exp,
                       scale=-1.0), tick=True)
        wk3 = P.t("scalar")
        E("vector", lambda e:
          e.tensor_scalar(out=sb["kq_c"], in0=sb["tkb"], scalar1=sb["tka"],
                          scalar2=127.0 * ISQI, op0=OP.mult, op1=OP.mult),
          waits=[wk3], tick=True)
        E("vector", lambda e:
          e.memset(sb["nhz_c"], 0.0), tick=True)
        for it in range(16):
            par = it % 2
            if par in w_tposeB2:
                lw = [w_tposeB2[par]]
            else:
                lw = [w_g2sp[par], P.t("scalar"), P.t("vector")]
            E("gpsimd", lambda e, tc=tc, it=it, par=par:
              e.dma_start(out=gts[par], in_=g2_sp.ap()[tc, it]),
              waits=lw, dma=True)
            wldg = P.d("gpsimd")
            E("vector", lambda e, par=par:
              e.tensor_scalar(out=htfs[par], in0=gts[par],
                              scalar1=sb["kq_c"], scalar2=RND, op0=OP.mult,
                              op1=OP.add), waits=[wldg], tick=True)
            E("vector", lambda e, par=par:
              e.tensor_scalar(out=htfs[par], in0=htfs[par], scalar1=-RND,
                              scalar2=127.0, op0=OP.add, op1=OP.min),
              tick=True)
            E("vector", lambda e, par=par:
              e.tensor_scalar(out=gts[par], in0=htfs[par], scalar1=-128.0,
                              scalar2=0.0, op0=OP.max, op1=OP.add,
                              accum_out=sb["tkb"]), tick=True)
            E("vector", lambda e:
              e.tensor_tensor(out=sb["nhz_c"], in0=sb["nhz_c"],
                              in1=sb["tkb"], op=OP.add), tick=True)
            wq2 = P.t("vector")
            for q in range(4):
                tw = [wq2]
                if w_psbf[q % 2] is not None:
                    tw.append(w_psbf[q % 2])
                E("tensor", lambda e, q=q, par=par:
                  e.transpose(psbf[q % 2],
                              gts[par][:, 128 * q:128 * (q + 1)],
                              sb["ident_bf"]), waits=tw, tick=True)
                cw = [P.t("tensor")]
                if q == 0 and par in w_hstg_sp:
                    cw.append(w_hstg_sp[par])
                E("scalar", lambda e, q=q, par=par:
                  e.activation(out=hstgs[par][:, 128 * q:128 * q + 128],
                               in_=psbf[q % 2], func=AF.Copy),
                  waits=cw, tick=True)
                w_psbf[q % 2] = P.t("scalar")
            w_tposeB2[par] = P.t("tensor")
            dmov("sync", hq_sp.ap()[tc][:, 4 * it:4 * it + 4, :]
                 .rearrange("p a b -> p (a b)"), hstgs[par],
                 waits=[P.t("scalar")])
            w_hstg_sp[par] = P.d("sync")
        E("vector", lambda e, tc=tc:
          e.tensor_copy(out=sb["hz_all"][:, tc:tc + 1], in_=sb["nhz_c"]),
          waits=[P.d("sync")], tick=True)
    w_passb = P.t("vector")

    w2bufs = [w2s_band, w2s_bandB]
    w2flat = [b.rearrange("p a b -> p (a b)") for b in w2bufs]
    xrsC = [xr, x1t]
    ytmps = [ytmp, ytmpB]
    statn = [("mu_c", "sq_c", "va_c", "rstd_c"),
             ("mu_c2", "sq_c2", "va_c2", "rstd_c2")]
    w_mmw2 = {}
    w_bankC = None
    w_mmhq = None
    w_out_done = {}
    for tcp in range(NTC // 2):
        tcs = (2 * tcp, 2 * tcp + 1)
        hqw = [w_mmhq] if w_mmhq is not None else [P.t("tensor")]
        for ti, tc in enumerate(tcs):
            dmov("sync", hq_stg[ti].rearrange("p a b -> p (a b)"),
                 hq_sp.ap()[tc].rearrange("p a b -> p (a b)"),
                 waits=(hqw if ti == 0 else []))
        wldh = P.d("sync")
        dmov("sync", xrsC[0], x1_sp.ap()[tcs[0]],
             waits=[P.t("vector"), P.t("scalar")])
        dmov("sync", xrsC[1], x1_sp.ap()[tcs[1]])
        wxr = P.d("sync")
        for ib in range(16):
            wpar = ib % 2
            lw = [w_mmw2[wpar]] if wpar in w_mmw2 else [P.t("tensor")]
            dmov("gpsimd", w2flat[wpar].rearrange("p (a c) -> p a c",
                                                  a=4),
                 w2s_sp.ap()[4 * ib:4 * ib + 4].transpose([1, 0, 2]),
                 waits=lw)
            wldw = P.d("gpsimd")
            for ti in range(2):
                for j in range(4):
                    for dt in range(4):
                        first = (ib == 0 and ti == 0 and j == 0 and dt == 0)
                        if first:
                            mw = [wldh, wldw]
                            if w_bankC is not None:
                                mw.append(w_bankC)
                            if tcp == 0:
                                mw += [P.t("vector"), P.t("scalar")]
                        elif ti == 0 and j == 0 and dt == 0:
                            mw = [wldw]
                        else:
                            mw = ()
                        E("tensor", lambda e, ti=ti, j=j, dt=dt, ib=ib,
                          wpar=wpar:
                          e.matmul(ps[4 * ti + dt][:, 0:512],
                                   hq_stg[ti][:, 4 * ib + j, :],
                                   w2flat[wpar][:, 2048 * j + 512 * dt:
                                                2048 * j + 512 * dt + 512],
                                   start=(ib == 0 and j == 0),
                                   stop=(ib == 15 and j == 3)),
                          waits=mw, tick=(dt == 3))
            w_mmw2[wpar] = P.t("tensor")
        wy = P.t("tensor")
        w_mmhq = wy
        for ti in range(2):
            for dt in range(4):
                sl = slice(512 * dt, 512 * (dt + 1))
                ew = [wy]
                if ti in w_out_done:
                    ew.append(w_out_done[ti])
                if tcp == 0 and dt == 0:
                    ew += [P.t("scalar"), P.t("vector")]
                E("vector", lambda e, ti=ti, dt=dt, sl=sl, tc=tcs[ti]:
                  e.tensor_scalar(out=ytmps[ti][:, sl],
                                  in0=ps[4 * ti + dt][:, 0:512],
                                  scalar1=sb["hz_all"][:, tc:tc + 1],
                                  scalar2=sb["ysc2_all"][:, tc:tc + 1],
                                  op0=OP.subtract, op1=OP.mult),
                  waits=ew, tick=True)
        w_bankC = P.t("vector")
        for ti, tc in enumerate(tcs):
            mu, sq, va, rstd = statn[ti]
            for dt in range(4):
                sl = slice(512 * dt, 512 * (dt + 1))
                E("vector", lambda e, ti=ti, sl=sl:
                  e.tensor_tensor(out=ytmps[ti][:, sl],
                                  in0=ytmps[ti][:, sl],
                                  in1=b2rep_bf[:, sl], op=OP.add),
                  tick=True)
                E("vector", lambda e, ti=ti, sl=sl:
                  e.tensor_tensor(out=ytmps[ti][:, sl],
                                  in0=ytmps[ti][:, sl],
                                  in1=xrsC[ti][:, sl], op=OP.add),
                  waits=([wxr] if ti == 0 else []), tick=True)
            wyt = P.t("vector")
            E("vector", lambda e, ti=ti, mu=mu:
              e.tensor_reduce(sb[mu], ytmps[ti], AX.X, OP.add),
              waits=[wyt], tick=True)
            E("scalar", lambda e, ti=ti, sq=sq:
              e.activation(out=xrsC[ti], in_=ytmps[ti], func=AF.Square,
                           accum_out=sb[sq]), waits=[wyt], tick=True)
            wst = P.t("scalar")
            E("vector", lambda e, mu=mu:
              e.tensor_scalar(out=sb[mu], in0=sb[mu], scalar1=1.0 / HID,
                              scalar2=None, op0=OP.mult), waits=[wst],
              tick=True)
            E("vector", lambda e, mu=mu, va=va:
              e.scalar_tensor_tensor(out=sb[va], in0=sb[mu], scalar=-1.0,
                                     in1=sb[mu], op0=OP.mult, op1=OP.mult),
              tick=True)
            E("vector", lambda e, sq=sq, va=va:
              e.scalar_tensor_tensor(out=sb[va], in0=sb[sq],
                                     scalar=1.0 / HID, in1=sb[va],
                                     op0=OP.mult, op1=OP.add), tick=True)
            E("vector", lambda e, va=va:
              e.tensor_scalar(out=sb[va], in0=sb[va], scalar1=1e-5,
                              scalar2=None, op0=OP.add), tick=True)
            wva = P.t("vector")
            E("scalar", lambda e, va=va, rstd=rstd:
              e.activation(out=sb[rstd], in_=sb[va], func=AF.Ln),
              waits=[wva], tick=True)
            E("scalar", lambda e, rstd=rstd:
              e.activation(out=sb[rstd], in_=sb[rstd], func=AF.Exp,
                           scale=-0.5), tick=True)
            wrs = P.t("scalar")
            E("vector", lambda e, ti=ti, mu=mu, rstd=rstd:
              e.tensor_scalar(out=ytmps[ti], in0=ytmps[ti],
                              scalar1=sb[mu], scalar2=sb[rstd],
                              op0=OP.subtract, op1=OP.mult), waits=[wrs],
              tick=True)
            dmov("sync", out_t.ap().rearrange("(n p) d -> n p d", p=128)[tc],
                 ytmps[ti], waits=[P.t("vector")])
            w_out_done[ti] = P.d("sync")
    wfin = P.d("sync")
    E("vector", lambda e: e.engine_nop(), waits=[wfin, P.d("gpsimd")],
      tick=True)

    return _finish(nc, P, sems_holder=None)


def _finish(nc, P, sems_holder=None):
    import contextlib
    with contextlib.ExitStack() as ctx:
        sems = {}
        for nm in ("psync", "pgpsimd", "ptensor", "pvector", "pscalar",
                   "dsync", "dgpsimd", "dscalar"):
            sems[nm] = ctx.enter_context(nc.semaphore(nm))
        block = ctx.enter_context(nc.Block())

        def runner(eng_name):
            def run(e):
                dma_done = 0
                prog = 0
                for fn, waits, inc in P.ops[eng_name]:
                    if (eng_name in ("vector", "scalar") and inc is not None
                            and inc[0] == "p" + eng_name and prog > 0):
                        e.wait_ge(sems["p" + eng_name], prog)
                    if (eng_name in ("gpsimd", "sync", "scalar")
                            and inc is not None
                            and inc[0] == "d" + eng_name and dma_done > 0):
                        e.wait_ge(sems["d" + eng_name], dma_done)
                    for snm, val in waits:
                        if val > 0:
                            e.wait_ge(sems[snm], val)
                    ins = fn(e)
                    if inc is not None and ins is not None:
                        ins.then_inc(sems[inc[0]], inc[1])
                        if inc[0] == "d" + eng_name:
                            dma_done += inc[1]
                        elif inc[0] == "p" + eng_name:
                            prog += inc[1]
            return run

        block.sync(runner("sync"))
        block.gpsimd(runner("gpsimd"))
        block.tensor(runner("tensor"))
        block.vector(runner("vector"))
        block.scalar(runner("scalar"))
    return nc


_NC_CACHE = None


def kernel(**inputs):
    global _NC_CACHE
    if _NC_CACHE is None:
        _NC_CACHE = build_nc()
    nc = _NC_CACHE
    in_maps = [_core_host_inputs(inputs, c) for c in range(NCORE)]
    res = run_bass_kernel_spmd(nc, in_maps, core_ids=list(range(NCORE)))
    out = np.concatenate([res.results[c]["out"] for c in range(NCORE)], 0)
    return out.reshape(B, S, HID).astype(np.float32)


# version 11 (final)
# speedup vs baseline: 1.0280x; 1.0280x over previous
import numpy as np
import concourse.bass as bass
import concourse.mybir as mybir
from concourse.bass_utils import run_bass_kernel_spmd
from concourse.bass_core_rust import IndirectOffsetOnAxis

F32 = mybir.dt.float32
BF16 = mybir.dt.bfloat16
I32 = mybir.dt.int32
AF = mybir.ActivationFunctionType
OP = mybir.AluOpType
AX = mybir.AxisListType

B, S, HID = 2, 4096, 2048
SC = 128.0 ** -0.5
C = 32
NCH = S // C
SEG = 128
NSEG = S // SEG
CPS = SEG // C
NCORE = 8
TOK = B * S
OWN = TOK // NCORE
NTC = OWN // 128
INNER = 8192
RND = float(3 << 22)
DBG_CHUNK = 0
LNS = float(np.log(SC))
ISQH = float(HID) ** -0.5
ISQI = float(INNER) ** -0.5


def _consts():
    j = np.arange(128)
    taui, pi = j // 4, j % 4
    pb, taub = j // 32, j % 32
    hm = pi[:, None] == pi[None, :]
    gkk = (hm & (taui[:, None] < taui[None, :])).astype(np.float32) * (SC * SC)
    gq = (taub[:, None] <= np.arange(32)[None, :]).astype(np.float32) * SC
    pbb = pb[:, None] == pb[None, :]
    M1c = (pbb * ((taub[:, None] <= taub[None, :]).astype(np.float32)
                  - (taub[:, None] <= 15))).astype(np.float32)
    M5 = (pbb * ((taub[:, None] == taub[None, :]).astype(np.float32)
                 - (taub[:, None] <= 15))).astype(np.float32)
    M4 = (pbb * (taub[:, None] <= taub[None, :])).astype(np.float32)
    permib = ((pb[:, None] == pi[None, :]) &
              (taub[:, None] == taui[None, :])).astype(np.float32)
    indC = np.zeros((128, 4), np.float32)
    indE = np.zeros((128, 4), np.float32)
    for p in range(4):
        indC[32 * p:32 * p + 32, p] = 1.0
        indE[32 * p + 16:32 * p + 32, p] = 1.0
    selb = np.zeros((4, 4, 128, 128), np.float32)
    for v4 in range(4):
        for p in range(4):
            for ji in range(128):
                if ji % 4 == p:
                    selb[v4, p, 32 * v4 + (ji // 4), ji] = 1.0
    selb = np.ascontiguousarray(
        selb.reshape(16, 128, 128).transpose(1, 0, 2))
    ident = np.eye(128, dtype=np.float32)
    return dict(gkk_mask=gkk, gq_mask=gq, M1c=M1c, M5=M5, M4=M4,
                PERMib=permib, PERMbi=np.ascontiguousarray(permib.T),
                indC=indC, indE=indE, selb=selb, ident=ident,
                ones_col=np.ones((128, 1), np.float32),
                ones_row=np.ones((1, 128), np.float32),
                lns_col=np.full((128, 1), np.log(128.0 ** -0.5), np.float32))


CONSTS = {k: v.astype(np.float32) for k, v in _consts().items()}


def _core_host_inputs(d, c):
    h0 = 2 * c
    cs = slice(128 * h0, 128 * (h0 + 2))
    cw = lambda w: np.ascontiguousarray(
        w[cs, 0].reshape(2, 128, 4).transpose(1, 0, 2))
    cb = lambda b: np.ascontiguousarray(b[cs].reshape(2, 128).T)
    b0 = c // 4
    seq0 = 1024 * (c % 4)
    gi = np.zeros((NTC, NCORE, 2, 128), np.int32)
    for tc in range(NTC):
        for src in range(NCORE):
            for ht in range(2):
                p = 2 * b0 + ht
                gi[tc, src, ht] = ((src * 4 + p) * S + seq0 + 128 * tc
                                   + np.arange(128))
    gi = np.ascontiguousarray(gi.reshape(-1, 128).T)
    io = dict(
        x_own=np.ascontiguousarray(d['L0_attn_x'].reshape(TOK, HID)
                                   [OWN * c:OWN * (c + 1)]),
        Wq=np.ascontiguousarray(d['L0_attn_Wq'][:, cs]),
        Wk=np.ascontiguousarray(d['L0_attn_Wk'][:, cs]),
        Wv=np.ascontiguousarray(d['L0_attn_Wv'][:, cs]),
        Wag=np.ascontiguousarray(np.concatenate(
            [d['L0_attn_Wa'][:, cs], d['L0_attn_Wg'][:, cs]], 1)),
        Wb=np.ascontiguousarray(d['L0_attn_Wb'][:, h0:h0 + 2]),
        agb_row=np.ascontiguousarray(np.concatenate(
            [d['L0_attn_ba'][cs], np.zeros(256, np.float32)])[None, :]),
        bb_row=np.ascontiguousarray(d['L0_attn_bb'][None, h0:h0 + 2]),
        cqw=cw(d['L0_attn_cqw']), cqb=cb(d['L0_attn_cqb']),
        ckw=cw(d['L0_attn_ckw']), ckb=cb(d['L0_attn_ckb']),
        cvw=cw(d['L0_attn_cvw']), cvb=cb(d['L0_attn_cvb']),
        lnw_rep=np.ascontiguousarray(np.repeat(d['L0_attn_ln_w'][None], 128, 0)),
        lnb_rep=np.ascontiguousarray(np.repeat(d['L0_attn_ln_b'][None], 128, 0)),
        b1_rep=np.ascontiguousarray(np.repeat(d['L0_ffn_b1'][None], 128, 0)),
        b2_rep=np.ascontiguousarray(np.repeat(d['L0_ffn_b2'][None], 128, 0)),
        Wo=np.ascontiguousarray(d['L0_attn_Wo']),
        w1T=np.ascontiguousarray(d['L0_ffn_w1'].T),
        w2T=np.ascontiguousarray(d['L0_ffn_w2'].T),
        w1sl=np.ascontiguousarray(d['L0_ffn_w1'].T[:, 1024 * c:1024 * (c + 1)]),
        w2sl=np.ascontiguousarray(d['L0_ffn_w2'].T[1024 * c:1024 * (c + 1)]),
        og_idx=gi,
    )
    io.update(CONSTS)
    return io


def build_nc(debug=False):
    nc = bass.Bass("TRN2", target_bir_lowering=False)
    EI = dict(kind="ExternalInput")
    dram = {}
    shapes = dict(x_own=[OWN, HID], Wq=[HID, 256], Wk=[HID, 256],
                  Wv=[HID, 256], Wag=[HID, 512], Wb=[HID, 2],
                  agb_row=[1, 512], bb_row=[1, 2],
                  cqw=[128, 2, 4], cqb=[128, 2], ckw=[128, 2, 4],
                  ckb=[128, 2], cvw=[128, 2, 4], cvb=[128, 2],
                  lnw_rep=[128, 128], lnb_rep=[128, 128],
                  b1_rep=[128, INNER], b2_rep=[128, HID],
                  Wo=[HID, HID], w1T=[HID, INNER], w2T=[INNER, HID],
                  w1sl=[HID, 1024], w2sl=[1024, HID],
                  og_idx=[128, NTC * NCORE * 2])
    for k, sh in shapes.items():
        dram[k] = nc.dram_tensor(k, sh, I32 if k == "og_idx" else F32, **EI)
    for k, v in CONSTS.items():
        dram[k] = nc.dram_tensor(k, list(v.shape), F32, **EI)
    out_t = nc.dram_tensor("out", [OWN, HID], F32, kind="ExternalOutput")
    dbg_names = ("Kint_c", "Qint_c", "nla_c", "Rt_blk", "E1_blk", "E2_blk",
                 "AC_blk", "Af_blk", "Ainv_int", "Rt_int", "Q_int", "P_blk",
                 "o_blk", "Ktok_blk", "Sneg0", "S0T0", "bpos_int", "gbf_c",
                 "t1_blk", "t2_blk")
    dbg_t = {}
    if debug:
        for k in dbg_names:
            dbg_t[k] = nc.dram_tensor("dbg_" + k, [128, 128], F32,
                                      kind="ExternalOutput")

    cc_xt_in = nc.dram_tensor("cc_xt_in", [16, 128, OWN], BF16)
    cc_xt_out = nc.dram_tensor("cc_xt_out", [NCORE, 16, 128, OWN], BF16,
                               addr_space="Shared")
    cc_og_in = nc.dram_tensor("cc_og_in", [4, S, 128], BF16)
    cc_og_out = nc.dram_tensor("cc_og_out", [NCORE * 4 * S, 128], BF16,
                               addr_space="Shared")
    cc_sw_in = nc.dram_tensor("cc_sw_in", [1, 2], F32)
    cc_sw_out = nc.dram_tensor("cc_sw_out", [1, 2], F32)
    kint_sp = nc.dram_tensor("kint_sp", [NSEG, 128, 512], BF16)
    qint_sp = nc.dram_tensor("qint_sp", [NSEG, 128, 512], BF16)
    vch_sp = nc.dram_tensor("vch_sp", [NSEG, 128, 512], BF16)
    nlab_sp = nc.dram_tensor("nlab_sp", [NSEG, 128, 512], F32)
    gb_sp = nc.dram_tensor("gb_sp", [NSEG, 128, 512], BF16)
    x1_sp = nc.dram_tensor("x1_sp", [NTC, 128, HID], F32)
    wo_bf = nc.dram_tensor("wo_bf", [HID, HID], BF16)
    w2s_sp = nc.dram_tensor("w2s_sp", [64, 128, HID], BF16)
    hq_sp = nc.dram_tensor("hq_sp", [NTC, 128, 64, 128], BF16)
    g2_sp = nc.dram_tensor("g2_sp", [NTC, 16, 128, 512], BF16)

    sb = {}

    def SBal(name, shape, dtype=F32):
        sb[name] = nc.alloc_sbuf_tensor(name, list(shape), dtype).ap()
        return sb[name]

    for k in CONSTS:
        sb[k] = nc.alloc_sbuf_tensor("sb_" + k, list(CONSTS[k].shape),
                                     F32).ap()
    SBal("ident_bf", [128, 128], BF16)
    SBal("permbi_bf", [128, 128], BF16)
    SBal("onesr_bf", [1, 128], BF16)
    SBal("agb_bf", [1, 512], BF16)
    SBal("bb_bf", [1, 2], BF16)
    SBal("lnw_s", [128, 128], F32)
    SBal("lnb_s", [128, 128], F32)
    for k in ("cqw", "ckw", "cvw"):
        SBal(k + "_s", [128, 2, 4], F32)
    for k in ("cqb", "ckb", "cvb"):
        SBal(k + "_s", [128, 2], F32)
    SBal("swk", [128, 2], F32)
    SBal("ogi_s", [128, NTC * NCORE * 2], I32)
    for p in range(4):
        SBal(f"Sneg{p}", [128, 128], F32)
        SBal(f"S0T{p}", [128, 128], BF16)
    for k in ("Rt_blk", "t1_blk", "t2_blk", "o_blk", "o_t1", "o_t2",
              "cvt0", "cvt1", "cvt2", "cvt3", "g_c"):
        SBal(k, [128, 128], F32)
    for k in ("E1_blk", "E2_blk", "AC_blk", "Af_blk", "Ainv_int", "nla_c"):
        SBal(k + "0", [128, 128], F32)
        SBal(k + "1", [128, 128], F32)
    for k in ("Rt_int", "Q_int", "P_blk", "og_blk"):
        SBal(k, [128, 128], BF16)
    for k in ("Gblk", "Ktok_blk", "Kint_c", "Qint_c", "gbf_c"):
        SBal(k + "0", [128, 128], BF16)
        SBal(k + "1", [128, 128], BF16)
    SBal("gbf_c2", [128, 128], BF16)
    SBal("GqT0", [128, 32], BF16)
    SBal("GqT1", [128, 32], BF16)
    for k in ("ACf_c", "ACe_c"):
        SBal(k + "0", [128, 4], F32)
        SBal(k + "1", [128, 4], F32)
    for k in ("mu_c", "sq_c", "va_c", "rstd_c", "habs", "hssq", "tka", "tkb",
              "kq_c", "ysc_c", "nhz_c", "sw_par", "sw_par2",
              "mu_c2", "sq_c2", "va_c2", "rstd_c2", "tka1", "tkb1"):
        SBal(k, [128, 1], F32)
    for k in ("bpos_int", "btn_int"):
        SBal(k + "0", [128, 1], F32)
        SBal(k + "1", [128, 1], F32)
    SBal("nlb_all", [128, NSEG, 2, 2], F32)
    SBal("tails", [128, 40], BF16)
    SBal("zsum", [128, NTC], F32)
    SBal("ysc1_all", [128, NTC], F32)
    for k in ("habs8", "hssq8", "ysc2_all", "hz_all"):
        SBal(k, [128, NTC], F32)
    SBal("swv", [1, 2], F32)

    AB = SBal("arena_bf", [128, 68 * 1024], BF16)
    AFR = SBal("arena_f", [128, 7 * 1024], F32)

    class Arena:
        def __init__(self, ap, total):
            self.ap, self.total, self.off = ap, total, 0

        def take(self, n):
            a = self.ap[:, self.off:self.off + n]
            self.off += n
            assert self.off <= self.total, f"arena overflow {self.off}"
            return a

    r3 = lambda ap, d1: ap.rearrange("p (a b) -> p a b", a=d1)
    r4 = lambda ap, d1, d2: ap.rearrange("p (a b c) -> p a b c", a=d1, b=d2)

    a1b = Arena(AB, 68 * 1024)
    a1f = Arena(AFR, 7 * 1024)
    Wq_s = r3(a1b.take(4096), 16)
    Wk_s = r3(a1b.take(4096), 16)
    Wv_s = r3(a1b.take(4096), 16)
    Wag_s = r3(a1b.take(8192), 16)
    Wb_s = r3(a1b.take(32), 16)
    xT_seg = r4(a1b.take(4096), 2, 16)
    xT_segB = r4(a1b.take(4096), 2, 16)
    xb_t = a1b.take(2048)
    xTst = r3(a1b.take(2048), 16)
    uextt = r3(a1b.take(12 * 131), 12)
    Kint_seg = r4(a1b.take(512), CPS, 32)
    Qint_seg = r4(a1b.take(512), CPS, 32)
    Vch_seg = r4(a1b.take(512), 2, 2)
    Vch_segB = r4(a1b.take(512), 2, 2)
    nla_tok = r4(a1f.take(512), 2, 2)
    g_tok = r4(a1f.take(512), 2, 2)
    p3b = Arena(AB, 68 * 1024)
    p3f = Arena(AFR, 7 * 1024)
    og_own = p3b.take(2048)
    ogT = r3(p3b.take(2048), 16)
    Wo_band = r3(p3b.take(8192), 16)
    xqT_all = r4(p3b.take(16384), 16, NTC)
    w1s = r3(p3b.take(8192), 16)
    wldb = p3b.take(2048)
    hgel = p3b.take(512)
    hq_t = p3b.take(512)
    w2s_band = r3(p3b.take(8192), 16)
    xq_bf = p3b.take(2048)
    b1rep_bf = p3b.take(INNER)
    b2rep_bf = p3b.take(HID)
    hstg = p3b.take(512)
    xr = p3f.take(2048)
    x1t = p3f.take(2048)
    ytmp = p3f.take(2048)
    htf = p3f.take(512)
    htf2 = p3f.take(512)

    _OG, _OGT, _WOB, _XQT, _W1S, _WLDB = 0, 2048, 4096, 12288, 28672, 36864
    _XQBF, _B1R = 48128, 50176
    assert p3b.off == 60928, p3b.off
    og_ownB = AB[:, 60928:62976]
    ogTB = r3(AB[:, 62976:65024], 16)
    hstgB = AB[:, 65024:65536]
    ytmpB = AB[:, 65536:69632].bitcast(F32)
    Wo_bandB = r3(AB[:, _W1S:_W1S + 8192], 16)
    xrB = AB[:, _B1R:_B1R + 4096].bitcast(F32)
    w1sB = r3(AB[:, _WOB:_WOB + 8192], 16)
    wldbB = AB[:, _XQBF:_XQBF + 2048]
    hq_stg = [r3(AB[:, _XQT:_XQT + 8192], 64),
              r3(AB[:, _XQT + 8192:_XQT + 16384], 64)]
    w2s_bandB = r3(AB[:, _W1S:_W1S + 8192], 16)

    psumh = [nc.alloc_psum_tensor(f"ps{i}", [128, 512], F32) for i in range(8)]
    ps = [t.ap() for t in psumh]
    psW = ps[0][:, 0:128]
    psG = ps[1][:, 0:128]
    psGq = ps[1][:, 128:160]
    psNC = ps[1][:, 160:164]
    psNE = ps[1][:, 164:168]
    psBI = ps[1][:, 168:169]
    psSW = ps[1][:, 169:171]
    psL1 = ps[2][:, 0:128]
    psL5 = ps[2][:, 128:256]
    psL4 = ps[2][:, 256:384]
    psNI = ps[2][:, 384:512]
    psR0 = ps[6][:, 0:128]
    psQ0 = ps[6][:, 128:256]
    psVt = ps[3][:, 256:384]
    psKt = ps[3][:, 384:512]
    psRI = ps[4][:, 0:128]
    psPB = ps[4][:, 128:256]
    psGP = ps[4][:, 256:384]
    psST = ps[4][:, 384:512]
    psPK = ps[5]
    psA = ps[6]
    psB_ = ps[7]
    psbf = [ps[i][:, 0:64].bitcast(BF16) for i in range(8)]
    psVtb = ps[3][:, 256:320].bitcast(BF16)
    psKtb = ps[3][:, 384:448].bitcast(BF16)

    class Plan:
        ENGS = ("sync", "gpsimd", "tensor", "vector", "scalar")

        def __init__(self):
            self.ops = {e: [] for e in self.ENGS}
            self.cnt = {e: 0 for e in self.ENGS}
            self.dma = {e: 0 for e in ("sync", "gpsimd", "scalar")}

        def emit(self, eng, fn, waits=(), tick=False, dma=False):
            if eng in ("vector", "scalar") and not dma:
                tick = True
            inc = None
            if tick:
                self.cnt[eng] += 1
                inc = ("p" + eng, 1)
            if dma:
                self.dma[eng] += 16
                inc = ("d" + eng, 16)
            self.ops[eng].append(
                (fn, tuple(w for w in waits if w is not None), inc))

        def t(self, eng):
            return ("p" + eng, self.cnt[eng])

        def d(self, eng):
            return ("d" + eng, self.dma[eng])

    P = Plan()
    E = P.emit

    def dmov(eng, dst, src, waits=()):
        E(eng, lambda e, dst=dst, src=src: e.dma_start(out=dst, in_=src),
          waits=waits, dma=True)

    for k in CONSTS:
        dmov("sync", sb[k], dram[k].ap())
    dmov("sync", sb["lnw_s"], dram["lnw_rep"].ap())
    dmov("sync", sb["lnb_s"], dram["lnb_rep"].ap())
    for k in ("cqw", "ckw", "cvw", "cqb", "ckb", "cvb"):
        dmov("sync", sb[k + "_s"], dram[k].ap())
    dmov("sync", sb["ogi_s"], dram["og_idx"].ap())
    dmov("gpsimd", sb["ident_bf"], dram["ident"].ap())
    dmov("gpsimd", sb["permbi_bf"], dram["PERMbi"].ap())
    dmov("gpsimd", sb["onesr_bf"], dram["ones_row"].ap())
    dmov("gpsimd", sb["agb_bf"], dram["agb_row"].ap())
    dmov("gpsimd", sb["bb_bf"], dram["bb_row"].ap())
    for dst, src in ((Wq_s, "Wq"), (Wk_s, "Wk"), (Wv_s, "Wv"),
                     (Wag_s, "Wag"), (Wb_s, "Wb")):
        dmov("gpsimd", dst, dram[src].ap().rearrange("(n p) c -> p n c", p=128))
    w_lds = P.d("sync")
    w_ldg = P.d("gpsimd")

    for p in range(4):
        E("vector", lambda e, p=p: e.memset(sb[f"Sneg{p}"], 0.0), tick=True)
        E("vector", lambda e, p=p: e.memset(sb[f"S0T{p}"], 0.0), tick=True)
    E("vector", lambda e: e.memset(sb["Q_int"], 0.0), tick=True)
    E("vector", lambda e: e.memset(sb["tails"], 0.0), waits=[w_ldg], tick=True)

    x_own_r = dram["x_own"].ap().rearrange("(n p) d -> n p d", p=128)
    for i in range(NTC):
        dmov("gpsimd", xb_t, x_own_r[i], waits=[P.t("scalar"), w_lds])
        wld = P.d("gpsimd")
        for dd in range(16):
            E("tensor", lambda e, dd=dd:
              e.transpose(psbf[dd % 2], xb_t[:, 128 * dd:128 * (dd + 1)],
                          sb["ident_bf"]),
              waits=[wld, P.t("scalar")], tick=True)
            E("scalar", lambda e, dd=dd:
              e.activation(out=xTst[:, dd, :], in_=psbf[dd % 2],
                           func=AF.Copy),
              waits=[P.t("tensor"), P.d("sync") if dd == 0 else None],
              tick=True)
        dmov("sync", cc_xt_in.ap()[:, :, 128 * i:128 * (i + 1)]
             .transpose([1, 0, 2]), xTst, waits=[P.t("scalar")])
    E("gpsimd", lambda e:
      e.collective_compute("AllGather", OP.bypass,
                           replica_groups=[list(range(NCORE))],
                           ins=[cc_xt_in.ap().opt()],
                           outs=[cc_xt_out.ap().opt()]),
      waits=[P.d("sync"), P.d("gpsimd")], tick=True)
    w_agx = P.t("gpsimd")

    w1sl_r = dram["w1sl"].ap().rearrange("(n p) c -> n p c", p=128)
    w2sl_r = dram["w2sl"].ap().rearrange("(n p) c -> n p c", p=128)
    for wr, nt, fd, dst in ((w1sl_r, 16, 1024, "sw_par"),
                            (w2sl_r, 8, HID, "sw_par2")):
        first = True
        for k in range(nt):
            for q in range(fd // 128):
                dmov("gpsimd", sb["t1_blk"], wr[k][:, 128 * q:128 * (q + 1)],
                     waits=[P.t("vector"), w_agx])
                wld = P.d("gpsimd")
                E("vector", lambda e:
                  e.tensor_reduce(sb["tka"], sb["t1_blk"], AX.X, OP.add,
                                  apply_absolute_value=True),
                  waits=[wld], tick=True)
                if first:
                    E("vector", lambda e, dst=dst:
                      e.tensor_copy(out=sb[dst], in_=sb["tka"]), tick=True)
                    first = False
                else:
                    E("vector", lambda e, dst=dst:
                      e.tensor_tensor(out=sb[dst], in0=sb[dst],
                                      in1=sb["tka"], op=OP.add), tick=True)
    E("tensor", lambda e:
      e.matmul(psSW[0:1, 0:1], sb["sw_par"], sb["ones_col"],
               start=True, stop=True), waits=[P.t("vector")], tick=True)
    E("tensor", lambda e:
      e.matmul(psSW[0:1, 1:2], sb["sw_par2"], sb["ones_col"],
               start=True, stop=True), tick=True)
    E("scalar", lambda e:
      e.activation(out=sb["swv"], in_=ps[1][0:1, 169:171], func=AF.Copy),
      waits=[P.t("tensor")], tick=True)
    dmov("sync", cc_sw_in.ap(), sb["swv"], waits=[P.t("scalar")])
    E("gpsimd", lambda e:
      e.collective_compute("AllReduce", OP.add,
                           replica_groups=[list(range(NCORE))],
                           ins=[cc_sw_in.ap().opt()],
                           outs=[cc_sw_out.ap().opt()]),
      waits=[P.d("sync")], tick=True)
    dmov("sync", sb["swv"], cc_sw_out.ap(), waits=[P.t("gpsimd")])
    E("tensor", lambda e:
      e.matmul(ps[1][:, 169:171].opt(), sb["ones_row"][0:1, :], sb["swv"],
               start=True, stop=True), waits=[P.d("sync")], tick=True)
    E("scalar", lambda e:
      e.activation(out=sb["swk"], in_=ps[1][:, 169:171], func=AF.Copy),
      waits=[P.t("tensor")], tick=True)

    def seg_tokens(b, s):
        t0 = b * S + SEG * s
        return t0 // OWN, t0 % OWN

    w_gb = {}
    w_tap_done = {}
    w_copy_done = {}
    w_silu = {}
    w_xt_used = {}
    w_kqv_free = None
    w_ag_evac = None
    w_ntok_free = None
    xT_segs = [xT_seg, xT_segB]
    cvts = [(sb["cvt0"], sb["cvt1"]), (sb["cvt2"], sb["cvt3"])]
    for s in range(NSEG):
        spar = s % 2
        xTc = xT_segs[spar]
        lw = [w_agx, P.t("tensor")] if spar not in w_xt_used else \
             [w_xt_used[spar]]
        for b in range(2):
            oc, col = seg_tokens(b, s)
            dmov("sync", xTc[:, b, :, :],
                 cc_xt_out.ap()[oc, :, :, col:col + SEG].transpose([1, 0, 2]),
                 waits=(lw if b == 0 else []))
        w_xt = P.d("sync")
        first_silu = True
        for b in range(2):
            for ht in range(2):
                pbank = ps[2 * b + ht]
                projs = ((Wq_s, "cqw_s", "cqb_s"), (Wk_s, "ckw_s", "ckb_s"),
                         (Wv_s, "cvw_s", "cvb_s"))
                mw = [w_xt]
                if (b, ht) in w_copy_done:
                    mw.append(w_copy_done[(b, ht)])
                if s == 0:
                    mw += [w_ldg, P.t("scalar"), P.t("vector")]
                for ti, (Wsb, cwn, cbn) in enumerate(projs):
                    pslot = pbank[:, 128 * ti:128 * ti + 128]
                    for dd in range(16):
                        E("tensor", lambda e, Wsb=Wsb, dd=dd, ht=ht,
                          pslot=pslot, xTc=xTc, b=b, ti=ti:
                          e.matmul(pslot,
                                   Wsb[:, dd, 128 * ht:128 * ht + 128],
                                   xTc[:, b, dd, :],
                                   start=(ti == 0 and dd == 0),
                                   stop=(ti == 2 and dd == 15)),
                          waits=(mw if (ti == 0 and dd == 0) else ()),
                          tick=(dd == 15))
                wmm = P.t("tensor")
                for ti, (Wsb, cwn, cbn) in enumerate(projs):
                    pslot = pbank[:, 128 * ti:128 * ti + 128]
                    tix = (ti * 2 + b) * 2 + ht
                    cpar = tix % 2
                    cw = [wmm]
                    if tix in w_tap_done:
                        cw.append(w_tap_done[tix])
                    E("scalar", lambda e, pslot=pslot, tix=tix:
                      e.activation(out=uextt[:, tix, 3:3 + SEG], in_=pslot,
                                   func=AF.Copy), waits=cw, tick=True)
                    w_copy_done[(b, ht)] = P.t("scalar")
                    wcp = P.t("scalar")
                    E("vector", lambda e, tix=tix:
                      e.tensor_copy(out=uextt[:, tix, 0:3],
                                    in_=sb["tails"][:, 3 * tix:3 * tix
                                                    + 3]),
                      waits=[wcp], tick=True)
                    E("vector", lambda e, tix=tix:
                      e.tensor_copy(out=sb["tails"][:, 3 * tix:3 * tix + 3],
                                    in_=uextt[:, tix, SEG:SEG + 3]),
                      tick=True)
                    cv = cvts[cpar]
                    E("vector", lambda e, cwn=cwn, ht=ht, tix=tix, cv=cv:
                      e.tensor_scalar(out=cv[0], in0=uextt[:, tix, 0:SEG],
                                      scalar1=sb[cwn][:, ht, 0:1],
                                      scalar2=None, op0=OP.mult), tick=True)
                    for jj in (1, 2, 3):
                        jw = ([w_silu[cpar]]
                              if jj == 1 and cpar in w_silu else [])
                        E("vector", lambda e, cwn=cwn, ht=ht, jj=jj, cv=cv,
                          tix=tix:
                          e.scalar_tensor_tensor(
                              out=cv[jj % 2],
                              in0=uextt[:, tix, jj:jj + SEG],
                              scalar=sb[cwn][:, ht, jj:jj + 1],
                              in1=cv[(jj + 1) % 2], op0=OP.mult,
                              op1=OP.add),
                          waits=jw, tick=True)
                    wcv = P.t("vector")
                    w_tap_done[tix] = wcv
                    sw = [wcv]
                    if first_silu and w_kqv_free is not None:
                        sw.append(w_kqv_free)
                    first_silu = False
                    pp = 2 * b + ht
                    if ti < 2:
                        dstKQ = Qint_seg if ti == 0 else Kint_seg
                        E("scalar", lambda e, cbn=cbn, ht=ht, dstKQ=dstKQ,
                          pp=pp, cv=cv:
                          e.activation(out=dstKQ[:, :, :, pp].opt(),
                                       in_=cv[1], func=AF.Silu,
                                       bias=sb[cbn][:, ht:ht + 1]),
                          waits=sw, tick=True)
                    else:
                        E("scalar", lambda e, cbn=cbn, b=b, ht=ht, cv=cv:
                          e.activation(out=Vch_seg[:, b, ht, :],
                                       in_=cv[1], func=AF.Silu,
                                       bias=sb[cbn][:, ht:ht + 1]),
                          waits=sw, tick=True)
                    w_silu[cpar] = P.t("scalar")
        wkqv = P.t("scalar")
        wmm_ab = []
        for b in range(2):
            pag = psA if b == 0 else psB_
            pbt = psPK[:, 2 * b:2 * b + 2]
            mw = [w_xt]
            if w_ag_evac is not None:
                mw.append(w_ag_evac)
            if s == 0:
                mw += [P.t("vector"), P.t("scalar")]
            for dd in range(16):
                E("tensor", lambda e, b=b, dd=dd, pag=pag, xTc=xTc:
                  e.matmul(pag[:, 0:512], xTc[:, b, dd, :],
                           Wag_s[:, dd, :], start=(dd == 0), stop=False),
                  waits=(mw if dd == 0 else ()),
                  tick=False)
            E("tensor", lambda e, b=b, pag=pag:
              e.matmul(pag[:, 0:512], sb["onesr_bf"], sb["agb_bf"],
                       start=False, stop=True), tick=True)
            for dd in range(16):
                E("tensor", lambda e, b=b, dd=dd, pbt=pbt, xTc=xTc:
                  e.matmul(pbt, xTc[:, b, dd, :], Wb_s[:, dd, :],
                           start=(b == 0 and dd == 0), stop=False),
                  tick=False)
            E("tensor", lambda e, b=b, pbt=pbt:
              e.matmul(pbt, sb["onesr_bf"], sb["bb_bf"], start=False,
                       stop=(b == 1)), tick=True)
            wmm_ab.append(P.t("tensor"))
        w_xt_used[spar] = P.t("tensor")
        for b in range(2):
            pag = psA if b == 0 else psB_
            pbt = psPK[:, 2 * b:2 * b + 2]
            ew = [wmm_ab[b]]
            if b == 0 and w_ntok_free is not None:
                ew += list(w_ntok_free)
            E("scalar", lambda e, b=b, pag=pag:
              e.activation(out=nla_tok[:, b, :, :].opt(), in_=pag[:, 0:256],
                           func=AF.Sigmoid), waits=ew, tick=True)
            E("scalar", lambda e, b=b, pag=pag:
              e.activation(out=g_tok[:, b, :, :].opt(), in_=pag[:, 256:512],
                           func=AF.Sigmoid), tick=True)
            E("scalar", lambda e, b=b, s=s, pbt=pbt:
              e.activation(out=sb["nlb_all"][:, s, b, :], in_=pbt,
                           func=AF.Sigmoid),
              waits=([wmm_ab[1]] if b == 0 else []), tick=True)
        for b in range(2):
            E("scalar", lambda e, b=b:
              e.activation(out=nla_tok[:, b, :, :].opt(),
                           in_=nla_tok[:, b, :, :].opt(), func=AF.Ln),
              tick=True)
            E("scalar", lambda e, b=b, s=s:
              e.activation(out=sb["nlb_all"][:, s, b, :],
                           in_=sb["nlb_all"][:, s, b, :], func=AF.Ln),
              tick=True)
        w_ag_evac = P.t("scalar")
        w_evac = w_ag_evac
        dmov("sync", kint_sp.ap()[s],
             Kint_seg.rearrange("p a b c -> p (a b c)"), waits=[wkqv])
        dmov("sync", qint_sp.ap()[s],
             Qint_seg.rearrange("p a b c -> p (a b c)"), waits=[wkqv])
        dmov("sync", vch_sp.ap()[s],
             Vch_seg.rearrange("p a b c -> p (a b c)"), waits=[wkqv])
        w_kqv_free = P.d("sync")
        for cci in range(CPS):
            dmov("sync",
                 nlab_sp.ap()[s][:, 128 * cci:128 * cci + 128]
                 .rearrange("(b h t) v -> t b h v", b=2, h=2),
                 nla_tok[32 * cci:32 * cci + 32], waits=[w_evac])
            dmov("gpsimd",
                 gb_sp.ap()[s][:, 128 * cci:128 * cci + 128]
                 .rearrange("(b h t) v -> t b h v", b=2, h=2),
                 g_tok[32 * cci:32 * cci + 32],
                 waits=([w_evac] if cci == 0 else []))
        w_gb[s] = P.d("gpsimd")
        w_ntok_free = (P.d("sync"), P.d("gpsimd"))
    w_p1 = P.d("sync")


    dmov("gpsimd", wo_bf.ap(), dram["Wo"].ap())
    w_wocast = P.d("gpsimd")

    Vch_segs = [Vch_seg, Vch_segB]
    w_vch_ld = {}
    w_vch_used = {}
    dmov("sync", Vch_segs[0].rearrange("p a b c -> p (a b c)"),
         vch_sp.ap()[0], waits=[w_p1])
    w_vch_ld[0] = P.d("sync")
    PC = [None, None]
    CC = {}
    nlbf = sb["nlb_all"].rearrange("p a b c -> p (a b c)")

    def emit_prep(cg):
        par = cg % 2
        sx = str(par)
        s, cci = cg // CPS, cg % CPS
        v4 = cg % 4
        pc = {}
        PC[par] = pc
        Kb, Qb = sb["Kint_c" + sx], sb["Qint_c" + sx]
        Kc = lambda p: Kb.rearrange("p (b c) -> p b c", b=32)[:, :, p]
        Qc = lambda p: Qb.rearrange("p (b c) -> p b c", b=32)[:, :, p]
        kw = CC.get(("KQz", par), [w_p1, P.t("tensor")])
        dmov("sync", Kb, kint_sp.ap()[s][:, 128 * cci:128 * cci + 128],
             waits=kw)
        dmov("sync", Qb, qint_sp.ap()[s][:, 128 * cci:128 * cci + 128])
        dmov("sync", sb["nla_c" + sx],
             nlab_sp.ap()[s][:, 128 * cci:128 * cci + 128],
             waits=CC.get(("nla_used", par), [P.t("tensor")]))
        gx = str(cg % 3)
        pc["gx"] = gx
        dmov("sync", sb["gbf_c" + gx],
             gb_sp.ap()[s][:, 128 * cci:128 * cci + 128],
             waits=CC.get(("gbf_used", cg % 3), [P.t("vector")])
             + [w_gb[s]])
        pc["w_ld"] = P.d("sync")
        if cci == CPS - 1 and s + 1 < NSEG:
            vp = (s + 1) % 2
            dmov("sync",
                 Vch_segs[vp].rearrange("p a b c -> p (a b c)"),
                 vch_sp.ap()[s + 1],
                 waits=w_vch_used.get(vp, [P.t("tensor")]))
            w_vch_ld[vp] = P.d("sync")
        yield
        bw = CC.get("ps1_free", [P.t("scalar"), P.t("vector")])
        for p in range(4):
            E("tensor", lambda e, v4=v4, p=p, s=s:
              e.matmul(psBI, sb["selb"][:, 4 * v4 + p, :],
                       nlbf[:, 4 * s + p:4 * s + p + 1],
                       start=(p == 0), stop=(p == 3)),
              waits=(bw if p == 0 else ()), tick=(p == 3))
        wBI = P.t("tensor")
        yield
        E("scalar", lambda e, sx=sx:
          e.activation(out=sb["bpos_int" + sx], in_=psBI, func=AF.Exp),
          waits=[wBI] + CC.get(("par_v", par), []), tick=True)
        w_bpos_s = P.t("scalar")
        E("vector", lambda e, sx=sx:
          e.tensor_scalar(out=sb["btn_int" + sx], in0=sb["bpos_int" + sx],
                          scalar1=-1.0, scalar2=None, op0=OP.mult),
          waits=[w_bpos_s], tick=True)
        yield
        E("tensor", lambda e, Kb=Kb:
          e.matmul(psG, Kb, Kb, start=True, stop=True),
          waits=[pc["w_ld"], w_bpos_s], tick=True)
        yield
        for p in range(4):
            E("tensor", lambda e, p=p, Kc=Kc, Qc=Qc:
              e.matmul(psGq[32 * p:32 * p + 32, :], Kc(p), Qc(p),
                       start=True, stop=True, tile_position=(0, 32 * p)),
              tick=(p == 3))
        wG = P.t("tensor")
        yield
        E("vector", lambda e, sx=sx:
          e.tensor_tensor(out=sb["Gblk" + sx], in0=psG, in1=sb["gkk_mask"],
                          op=OP.mult),
          waits=[wG] + CC.get(("par_t", par), []), tick=True)
        pc["w_gblk"] = P.t("vector")
        E("vector", lambda e, sx=sx:
          e.tensor_tensor(out=sb["GqT" + sx], in0=psGq, in1=sb["gq_mask"],
                          op=OP.mult), tick=True)
        pc["w_gqt"] = P.t("vector")
        wGe = pc["w_gqt"]
        yield
        cw = [pc["w_ld"]] + CC.get("exps_s", [])
        for i, (dst, M) in enumerate(((psL1, "M1c"), (psL5, "M5"),
                                      (psL4, "M4"), (psNI, "PERMib"))):
            E("tensor", lambda e, dst=dst, M=M, sx=sx:
              e.matmul(dst, sb[M], sb["nla_c" + sx], start=True, stop=True),
              waits=(cw if i == 0 else ()), tick=(i == 3))
        yield
        E("tensor", lambda e, sx=sx:
          e.matmul(psNC, sb["nla_c" + sx], sb["indC"], start=True,
                   stop=True), waits=[wGe], tick=True)
        E("tensor", lambda e, sx=sx:
          e.matmul(psNE, sb["nla_c" + sx], sb["indE"], start=True,
                   stop=True), tick=True)
        wLA = P.t("tensor")
        CC[("nla_used", par)] = [wLA]
        yield
        first = [wLA] + CC.get(("par_v", par), [])
        for i, (dst, psrc, kw2) in enumerate((
                ("E2_blk", psL1, dict(scale=-1.0)),
                ("AC_blk", psL1, {}),
                ("E1_blk", psL5, "LNS"),
                ("Af_blk", psL4, {}),
                ("Ainv_int", psNI, dict(scale=-1.0)),
                ("ACf_c", psNC, {}),
                ("ACe_c", psNE, {}))):
            if kw2 == "LNS":
                E("scalar", lambda e, dst=dst, psrc=psrc, sx=sx:
                  e.activation(out=sb[dst + sx], in_=psrc, func=AF.Exp,
                               scale=-1.0, bias=sb["lns_col"][:, 0:1]),
                  tick=True)
            else:
                E("scalar", lambda e, dst=dst, psrc=psrc, kw2=kw2, sx=sx:
                  e.activation(out=sb[dst + sx], in_=psrc, func=AF.Exp,
                               **kw2),
                  waits=(first if i == 0 else ()), tick=True)
        pc["w_exps"] = P.t("scalar")
        CC["exps_s"] = [pc["w_exps"]]
        yield
        vp = s % 2
        tw = (CC.get("t2v", [P.t("vector")])
              + CC.get("ktok_s", [P.t("scalar")]) + [w_vch_ld[vp]])
        for p in range(4):
            E("tensor", lambda e, p=p, cci=cci, vp=vp:
              e.transpose(psVtb[32 * p:32 * p + 32, :],
                          Vch_segs[vp][:, p // 2, p % 2,
                                       32 * cci:32 * cci + 32],
                          sb["ident_bf"], tile_position=(0, 32 * p)),
              waits=(tw if p == 0 else ()), tick=(p == 3))
        w_vch_used[vp] = [P.t("tensor")]
        yield
        for p in range(4):
            E("tensor", lambda e, p=p, Kc=Kc:
              e.transpose(psKtb[32 * p:32 * p + 32, :], Kc(p),
                          sb["ident_bf"], tile_position=(0, 32 * p)),
              tick=(p == 3))
        pc["w_vt"] = P.t("tensor")
        yield
        E("scalar", lambda e, sx=sx:
          e.activation(out=sb["Ktok_blk" + sx], in_=psKtb, func=AF.Copy,
                       scale=SC), waits=[pc["w_vt"]]
          + CC.get(("par_t", par), []), tick=True)
        pc["w_ktok"] = P.t("scalar")
        CC["ktok_s"] = [pc["w_ktok"]]
        CC["ps1_free"] = [pc["w_exps"], pc["w_gqt"]]
        yield

    gen0 = emit_prep(0)
    for _ in gen0:
        pass
    gnext = None
    for cg in range(NCH if not debug else DBG_CHUNK + 1):
        par = cg % 2
        sx = str(par)
        pc = PC[par]
        Kb, Qb = sb["Kint_c" + sx], sb["Qint_c" + sx]
        Kc = lambda p, Kb=Kb: Kb.rearrange("p (b c) -> p b c", b=32)[:, :, p]
        Qc = lambda p, Qb=Qb: Qb.rearrange("p (b c) -> p b c", b=32)[:, :, p]
        if not debug and cg + 1 < NCH:
            gnext = emit_prep(cg + 1)
        else:
            gnext = None
        rw = ([pc["w_ld"]] + CC.get("S0T_s", [P.t("scalar")])
              + CC.get("oev", [P.t("vector")]))
        for p in range(4):
            E("tensor", lambda e, p=p, Kc=Kc:
              e.matmul(psR0[32 * p:32 * p + 32, :], Kc(p), sb[f"S0T{p}"],
                       start=True, stop=True, tile_position=(0, 32 * p)),
              waits=(rw if p == 0 else ()), tick=(p == 3))
            E("tensor", lambda e, p=p, Qc=Qc:
              e.matmul(psQ0[32 * p:32 * p + 32, :], Qc(p), sb[f"S0T{p}"],
                       start=True, stop=True, tile_position=(0, 32 * p)),
              tick=(p == 3))
        wR0 = P.t("tensor")
        CC[("KQz", par)] = [wR0]
        E("vector", lambda e, sx=sx:
          e.scalar_tensor_tensor(out=sb["t1_blk"], in0=psR0, scalar=-1.0,
                                 in1=sb["E1_blk" + sx], op0=OP.mult,
                                 op1=OP.mult),
          waits=[wR0, pc["w_exps"]], tick=True)
        E("vector", lambda e, sx=sx:
          e.scalar_tensor_tensor(out=sb["t2_blk"], in0=psVtb, scalar=-1.0,
                                 in1=sb["E2_blk" + sx], op0=OP.mult,
                                 op1=OP.mult),
          waits=[pc["w_vt"]], tick=True)
        CC["t2v"] = [P.t("vector")]
        E("vector", lambda e:
          e.tensor_tensor(out=sb["Rt_blk"], in0=sb["t1_blk"],
                          in1=sb["t2_blk"], op=OP.add), tick=True)
        wRt = P.t("vector")
        E("tensor", lambda e:
          e.matmul(psRI, sb["PERMib"], sb["Rt_blk"], start=True, stop=True),
          waits=[wRt], tick=True)
        wRImm = P.t("tensor")
        E("scalar", lambda e, sx=sx:
          e.activation(out=sb["Rt_int"], in_=psRI, func=AF.Copy,
                       scale=sb["bpos_int" + sx][:, 0:1]),
          waits=[wRImm], tick=True)
        wRi = P.t("scalar")

        wstep_v = P.t("vector")
        E("tensor", lambda e, sx=sx:
          e.matmul(psW, sb["Gblk" + sx], sb["Rt_int"], start=True,
                   stop=True),
          waits=[wRi, pc["w_gblk"], wstep_v] + CC.get("qz", []), tick=True)
        for t in range(C):
            E("tensor", lambda e, sx=sx, t=t:
              e.matmul(psW, sb["Gblk" + sx], sb["Q_int"], start=False,
                       stop=True, skip_group_check=True),
              waits=([] if t == 0 else [wstep_v]), tick=True)
            wmm = P.t("tensor")
            E("vector", lambda e, sx=sx:
              e.scalar_tensor_tensor(out=sb["Q_int"], in0=psW,
                                     scalar=sb["btn_int" + sx][:, 0:1],
                                     in1=sb["Ainv_int" + sx], op0=OP.mult,
                                     op1=OP.mult),
              waits=([wmm, pc["w_exps"]] if t == 0 else [wmm]), tick=True)
            wstep_v = P.t("vector")
            if gnext is not None:
                next(gnext, None)
            tr = CC.get("trail")
            if tr is not None:
                next(tr, None)
        if gnext is not None:
            for _ in gnext:
                pass
        if CC.get("trail") is not None:
            for _ in CC["trail"]:
                pass
            CC["trail"] = None

        def pnext():
            pass

        E("tensor", lambda e:
          e.matmul(psPB, sb["permbi_bf"], sb["Rt_int"], start=True,
                   stop=False), waits=[wstep_v], tick=False)
        E("tensor", lambda e:
          e.matmul(psPB, sb["permbi_bf"], sb["Q_int"], start=False,
                   stop=True), tick=True)
        pnext()
        wPBmm = P.t("tensor")
        E("scalar", lambda e:
          e.activation(out=sb["P_blk"], in_=psPB, func=AF.Copy),
          waits=[wPBmm], tick=True)
        wPb = P.t("scalar")
        E("vector", lambda e:
          e.memset(sb["Q_int"], 0.0), waits=[wPBmm], tick=True)
        CC["qz"] = [P.t("vector")]
        pnext()
        for p in range(4):
            E("tensor", lambda e, p=p, sx=sx:
              e.matmul(psGP[32 * p:32 * p + 32, :],
                       sb["GqT" + sx][32 * p:32 * p + 32, :],
                       sb["P_blk"][32 * p:32 * p + 32, :],
                       start=True, stop=True,
                       tile_position=(32 * p, 32 * p)),
              waits=([wPb, pc["w_gqt"]] if p == 0 else ()), tick=(p == 3))
            E("tensor", lambda e, p=p, sx=sx:
              e.matmul(psPK[:, 128 * p:128 * p + 128],
                       sb["P_blk"][32 * p:32 * p + 32, :],
                       sb["Ktok_blk" + sx][32 * p:32 * p + 32, :],
                       start=True, stop=True, tile_position=(32 * p, 0)),
              waits=([pc["w_ktok"]] if p == 0 else ()), tick=(p == 3))
        wOP = P.t("tensor")
        E("vector", lambda e, sx=sx:
          e.scalar_tensor_tensor(out=sb["o_t1"], in0=psQ0, scalar=-1.0,
                                 in1=sb["Af_blk" + sx], op0=OP.mult,
                                 op1=OP.mult), waits=[wOP], tick=True)
        E("vector", lambda e, sx=sx:
          e.scalar_tensor_tensor(out=sb["o_t2"], in0=psGP, scalar=-1.0,
                                 in1=sb["AC_blk" + sx], op0=OP.mult,
                                 op1=OP.mult), tick=True)
        CC["oev"] = [P.t("vector")]
        for p in range(4):
            E("scalar", lambda e, p=p, sx=sx:
              e.activation(out=sb["t1_blk"] if p % 2 == 0 else sb["t2_blk"],
                           in_=sb[f"Sneg{p}"], func=AF.Copy,
                           scale=sb["ACf_c" + sx][:, p:p + 1]),
              waits=([wOP, P.t("vector")] if p == 0 else [P.t("vector")]),
              tick=True)
            wS1 = P.t("scalar")
            E("vector", lambda e, p=p, sx=sx:
              e.scalar_tensor_tensor(out=sb[f"Sneg{p}"],
                                     in0=psPK[:, 128 * p:128 * p + 128],
                                     scalar=sb["ACe_c" + sx][:, p:p + 1],
                                     in1=sb["t1_blk"] if p % 2 == 0
                                     else sb["t2_blk"], op0=OP.mult,
                                     op1=OP.add), waits=[wS1], tick=True)
            wSU = P.t("vector")
            E("tensor", lambda e, p=p:
              e.transpose(psST, sb[f"Sneg{p}"], sb["ident"]),
              waits=[wSU, P.t("scalar")], tick=True)
            E("scalar", lambda e, p=p:
              e.activation(out=sb[f"S0T{p}"], in_=psST, func=AF.Copy),
              waits=[P.t("tensor")], tick=True)
            pnext()
        if gnext is not None:
            for _ in gnext:
                pass
        CC["S0T_s"] = [P.t("scalar")]
        CC[("par_v", par)] = [P.t("vector")]
        CC[("par_t", par)] = [P.t("tensor")]

        def emit_trail(cg=cg, par=par, sx=sx, gx=pc["gx"]):
            E("vector", lambda e:
              e.tensor_tensor(out=sb["o_blk"], in0=sb["o_t1"],
                              in1=sb["o_t2"], op=OP.add),
              waits=[P.t("scalar")], tick=True)
            E("vector", lambda e:
              e.tensor_reduce(sb["mu_c"], sb["o_blk"], AX.X, OP.add),
              tick=True)
            wLN = P.t("vector")
            yield
            E("scalar", lambda e:
              e.activation(out=sb["g_c"], in_=sb["o_blk"],
                           func=AF.Square, accum_out=sb["sq_c"]),
              waits=[wLN], tick=True)
            wSQ = P.t("scalar")
            yield
            E("vector", lambda e:
              e.tensor_scalar(out=sb["mu_c"], in0=sb["mu_c"],
                              scalar1=1.0 / 128, scalar2=None, op0=OP.mult),
              waits=[wSQ], tick=True)
            E("vector", lambda e:
              e.scalar_tensor_tensor(out=sb["va_c"], in0=sb["mu_c"],
                                     scalar=-1.0, in1=sb["mu_c"],
                                     op0=OP.mult, op1=OP.mult), tick=True)
            yield
            E("vector", lambda e:
              e.scalar_tensor_tensor(out=sb["va_c"], in0=sb["sq_c"],
                                     scalar=1.0 / 128, in1=sb["va_c"],
                                     op0=OP.mult, op1=OP.add), tick=True)
            E("vector", lambda e:
              e.tensor_scalar(out=sb["va_c"], in0=sb["va_c"], scalar1=1e-5,
                              scalar2=None, op0=OP.add), tick=True)
            wVA = P.t("vector")
            yield
            E("scalar", lambda e:
              e.activation(out=sb["rstd_c"], in_=sb["va_c"], func=AF.Ln),
              waits=[wVA], tick=True)
            E("scalar", lambda e:
              e.activation(out=sb["rstd_c"], in_=sb["rstd_c"], func=AF.Exp,
                           scale=-0.5), tick=True)
            wRS = P.t("scalar")
            yield
            E("vector", lambda e:
              e.scalar_tensor_tensor(out=sb["o_t1"], in0=sb["o_blk"],
                                     scalar=sb["mu_c"], in1=sb["lnw_s"],
                                     op0=OP.subtract, op1=OP.mult),
              waits=[wRS], tick=True)
            yield
            E("vector", lambda e:
              e.tensor_scalar(out=sb["o_t1"], in0=sb["o_t1"],
                              scalar1=sb["rstd_c"], scalar2=None,
                              op0=OP.mult), tick=True)
            E("vector", lambda e:
              e.tensor_tensor(out=sb["o_t2"], in0=sb["o_t1"],
                              in1=sb["lnb_s"], op=OP.add), tick=True)
            yield
            E("vector", lambda e, gx=gx:
              e.tensor_tensor(out=sb["og_blk"], in0=sb["o_t2"],
                              in1=sb["gbf_c" + gx], op=OP.mult),
              waits=CC.get("og_dma", []), tick=True)
            wOG = P.t("vector")
            CC[("gbf_used", cg % 3)] = [wOG]
            dmov("sync", cc_og_in.ap()[:, C * cg:C * cg + C, :],
                 sb["og_blk"], waits=[wOG])
            CC["og_dma"] = [P.d("sync")]
            yield

        CC["trail"] = emit_trail()
        if debug and cg == DBG_CHUNK:
            wall = [P.t("vector"), P.t("scalar"), P.t("tensor")]
            for k in dbg_names:
                k2 = k + "0" if (k + "0") in sb else k
                src_ = sb[k2]
                if src_.dtype != F32:
                    E("vector", lambda e, src_=src_:
                      e.tensor_copy(out=sb["t2_blk"][:, 0:src_.shape[1]],
                                    in_=src_),
                      waits=list(wall) + [P.d("gpsimd")])
                    E("gpsimd", lambda e, k=k, src_=src_: e.dma_start(
                        out=dbg_t[k].ap()[:, 0:src_.shape[1]],
                        in_=sb["t2_blk"][:, 0:src_.shape[1]],
                        allow_slow_non_contiguous=True),
                      waits=[P.t("vector")], dma=True)
                else:
                    E("gpsimd", lambda e, k=k, src_=src_: e.dma_start(
                        out=dbg_t[k].ap()[:, 0:src_.shape[1]], in_=src_,
                        allow_slow_non_contiguous=True),
                      waits=wall, dma=True)
    if debug:
        E("vector", lambda e: e.engine_nop(),
          waits=[P.d("gpsimd"), P.d("sync")])
        return _finish(nc, P, sems_holder=None)
    if CC.get("trail") is not None:
        for _ in CC["trail"]:
            pass
        CC["trail"] = None
    w_scan_s = P.d("sync")
    E("gpsimd", lambda e:
      e.collective_compute("AllGather", OP.bypass,
                           replica_groups=[list(range(NCORE))],
                           ins=[cc_og_in.ap().opt()],
                           outs=[cc_og_out.ap()
                                 .rearrange("(n a) c -> n a c", n=NCORE)
                                 .opt()]),
      waits=[w_scan_s, P.t("vector"), P.t("scalar")], tick=True)
    w_agog = P.t("gpsimd")

    ogi = sb["ogi_s"]

    def ogidx(tc, src, ht):
        k = (tc * NCORE + src) * 2 + ht
        return ogi[:, k:k + 1]

    Wo_r = wo_bf.ap().rearrange("(n p) c -> p n c", p=128)
    w1T_r = dram["w1T"].ap().rearrange("(n p) c -> p n c", p=128)
    w2T_r = dram["w2T"].ap().rearrange("(n p) c -> n p c", p=128)
    x_own_rr = dram["x_own"].ap().rearrange("(n p) d -> n p d", p=128)

    og_owns = [og_own, og_ownB]
    ogTs = [ogT, ogTB]
    Wo_bands = [Wo_band, Wo_bandB]
    x1ts = [x1t, ytmpB]
    xrs = [xr, xrB]
    w_psbf = {0: None, 1: None}
    w_tpose = None
    w_ogT_used = None
    w_mmWo = {}
    w_bankA = {}
    w_xr_free = None
    w_xqT_used = None
    w_x1_free = {}
    w_boot = [P.t("vector"), P.t("scalar")]
    for tcp in range(NTC // 2):
        pair = (2 * tcp, 2 * tcp + 1)
        gw0 = [w_agog] + ([w_tpose] if w_tpose is not None else [])
        wogs = []
        for ti, tc in enumerate(pair):
            for src in range(NCORE):
                for ht in range(2):
                    E("gpsimd", lambda e, tc=tc, src=src, ht=ht, ti=ti:
                      e.indirect_dma_start(
                          og_owns[ti][:, 256 * src + 128 * ht:
                                      256 * src + 128 * ht + 128],
                          None, cc_og_out.ap(),
                          IndirectOffsetOnAxis(ap=ogidx(tc, src, ht), axis=0)),
                      waits=(gw0 if (ti, src, ht) == (0, 0, 0) else []),
                      dma=True)
            wogs.append(P.d("gpsimd"))
        for ti in range(2):
            for dd in range(16):
                tw = [wogs[ti]]
                if w_psbf[dd % 2] is not None:
                    tw.append(w_psbf[dd % 2])
                if tcp == 0 and ti == 0 and dd < 2:
                    tw += w_boot
                E("tensor", lambda e, dd=dd, ti=ti:
                  e.transpose(psbf[dd % 2],
                              og_owns[ti][:, 128 * dd:128 * (dd + 1)],
                              sb["ident_bf"]), waits=tw, tick=True)
                cw = [P.t("tensor")]
                if ti == 0 and dd == 0 and w_ogT_used is not None:
                    cw.append(w_ogT_used)
                E("scalar", lambda e, dd=dd, ti=ti:
                  e.activation(out=ogTs[ti][:, dd, :], in_=psbf[dd % 2],
                               func=AF.Copy), waits=cw, tick=True)
                w_psbf[dd % 2] = P.t("scalar")
        w_tpose = P.t("tensor")
        wogT = P.t("scalar")
        xw = list(w_xr_free) if w_xr_free is not None else []
        dmov("sync", xrs[0], x_own_rr[pair[0]], waits=xw)
        dmov("sync", xrs[1], x_own_rr[pair[1]])
        wxr = P.d("sync")
        for band in range(4):
            bp = band % 2
            lw = [w_mmWo[bp]] if bp in w_mmWo else [P.t("tensor"),
                                                    w_wocast]
            dmov("sync", Wo_bands[bp],
                 Wo_r[:, :, 512 * band:512 * (band + 1)], waits=lw)
            wWo = P.d("sync")
            for ti in range(2):
                pband = psA if ti == 0 else psB_
                mw = [wWo, wogT]
                if ti in w_bankA:
                    mw.append(w_bankA[ti])
                if tcp == 0 and band == 0:
                    mw = mw + w_boot
                for chn in range(16):
                    E("tensor", lambda e, chn=chn, ti=ti, bp=bp, pband=pband:
                      e.matmul(pband, ogTs[ti][:, chn, :],
                               Wo_bands[bp][:, chn, :],
                               start=(chn == 0), stop=(chn == 15)),
                      waits=(mw if chn == 0 else ()), tick=(chn == 15))
                wA = P.t("tensor")
                ew = [wA, wxr]
                if band == 0 and ti in w_x1_free:
                    ew += list(w_x1_free[ti])
                E("vector", lambda e, band=band, ti=ti, pband=pband:
                  e.tensor_tensor(out=x1ts[ti][:, 512 * band:
                                               512 * (band + 1)],
                                  in0=pband,
                                  in1=xrs[ti][:, 512 * band:512 * (band + 1)],
                                  op=OP.add), waits=ew, tick=True)
                w_bankA[ti] = P.t("vector")
            w_mmWo[bp] = P.t("tensor")
        w_ogT_used = P.t("tensor")
        wx1 = P.t("vector")
        for ti, tc in enumerate(pair):
            x1c = x1ts[ti]
            scr = xrs[ti]
            E("vector", lambda e, x1c=x1c:
              e.tensor_reduce(sb["mu_c"], x1c, AX.X, OP.add),
              waits=[wx1], tick=True)
            E("scalar", lambda e, x1c=x1c, scr=scr:
              e.activation(out=scr, in_=x1c, func=AF.Square,
                           accum_out=sb["sq_c"]),
              waits=[wx1, P.t("vector")], tick=True)
            wst = P.t("scalar")
            E("vector", lambda e:
              e.tensor_scalar(out=sb["mu_c"], in0=sb["mu_c"],
                              scalar1=1.0 / HID, scalar2=None, op0=OP.mult),
              waits=[wst], tick=True)
            E("vector", lambda e:
              e.scalar_tensor_tensor(out=sb["va_c"], in0=sb["mu_c"],
                                     scalar=-1.0, in1=sb["mu_c"],
                                     op0=OP.mult, op1=OP.mult), tick=True)
            E("vector", lambda e:
              e.scalar_tensor_tensor(out=sb["va_c"], in0=sb["sq_c"],
                                     scalar=1.0 / HID, in1=sb["va_c"],
                                     op0=OP.mult, op1=OP.add), tick=True)
            E("vector", lambda e:
              e.tensor_scalar(out=sb["va_c"], in0=sb["va_c"], scalar1=1e-5,
                              scalar2=None, op0=OP.add), tick=True)
            wva = P.t("vector")
            E("scalar", lambda e:
              e.activation(out=sb["rstd_c"], in_=sb["va_c"], func=AF.Ln),
              waits=[wva], tick=True)
            E("scalar", lambda e:
              e.activation(out=sb["rstd_c"], in_=sb["rstd_c"], func=AF.Exp,
                           scale=-0.5), tick=True)
            wrs = P.t("scalar")
            E("vector", lambda e, x1c=x1c:
              e.tensor_scalar(out=x1c, in0=x1c, scalar1=sb["mu_c"],
                              scalar2=sb["rstd_c"], op0=OP.subtract,
                              op1=OP.mult), waits=[wrs], tick=True)
            wx1n = P.t("vector")
            dmov("sync", x1_sp.ap()[tc], x1c, waits=[wx1n])
            E("scalar", lambda e, x1c=x1c, scr=scr:
              e.activation(out=scr, in_=x1c, func=AF.Square,
                           accum_out=sb["sq_c"]), waits=[wx1n], tick=True)
            E("vector", lambda e, x1c=x1c:
              e.tensor_reduce(sb["habs"], x1c, AX.X, OP.max,
                              apply_absolute_value=True), waits=[wx1n],
              tick=True)
            E("vector", lambda e:
              e.tensor_scalar(out=sb["tka"], in0=sb["sq_c"],
                              scalar1=1.0 / HID, scalar2=1e-6, op0=OP.mult,
                              op1=OP.add), waits=[P.t("scalar")], tick=True)
            wtk = P.t("vector")
            E("scalar", lambda e:
              e.activation(out=sb["tka"], in_=sb["tka"], func=AF.Ln),
              waits=[wtk], tick=True)
            E("scalar", lambda e:
              e.activation(out=sb["tka"], in_=sb["tka"], func=AF.Exp,
                           scale=-0.5), tick=True)
            wk1 = P.t("scalar")
            E("vector", lambda e:
              e.tensor_scalar(out=sb["tkb"], in0=sb["habs"],
                              scalar1=sb["tka"], scalar2=ISQH, op0=OP.mult,
                              op1=OP.mult), waits=[wk1], tick=True)
            E("vector", lambda e:
              e.tensor_scalar(out=sb["tkb"], in0=sb["tkb"], scalar1=1e-5,
                              scalar2=None, op0=OP.max), tick=True)
            E("vector", lambda e, tc=tc:
              e.tensor_scalar(out=sb["ysc1_all"][:, tc:tc + 1],
                              in0=sb["tkb"], scalar1=sb["swk"][:, 0:1],
                              scalar2=1.0 / (127.0 * HID * INNER),
                              op0=OP.mult, op1=OP.mult), tick=True)
            wk2 = P.t("vector")
            E("scalar", lambda e:
              e.activation(out=sb["tkb"], in_=sb["tkb"], func=AF.Ln),
              waits=[wk2], tick=True)
            E("scalar", lambda e:
              e.activation(out=sb["tkb"], in_=sb["tkb"], func=AF.Exp,
                           scale=-1.0), tick=True)
            wk3 = P.t("scalar")
            E("vector", lambda e:
              e.tensor_scalar(out=sb["kq_c"], in0=sb["tkb"],
                              scalar1=sb["tka"], scalar2=127.0 * ISQH,
                              op0=OP.mult, op1=OP.mult),
              waits=[wk3], tick=True)
            E("vector", lambda e, x1c=x1c, scr=scr:
              e.tensor_scalar(out=scr, in0=x1c, scalar1=sb["kq_c"],
                              scalar2=None, op0=OP.mult), tick=True)
            E("vector", lambda e, scr=scr:
              e.tensor_scalar(out=scr, in0=scr, scalar1=RND, scalar2=-RND,
                              op0=OP.add, op1=OP.add), tick=True)
            E("vector", lambda e, scr=scr:
              e.tensor_scalar(out=scr, in0=scr, scalar1=127.0,
                              scalar2=-128.0, op0=OP.min, op1=OP.max),
              tick=True)
            aw = [w_xqT_used] if w_xqT_used is not None else []
            E("vector", lambda e, tc=tc, scr=scr:
              e.tensor_scalar(out=xq_bf, in0=scr, scalar1=0.0, scalar2=0.0,
                              op0=OP.add, op1=OP.add,
                              accum_out=sb["zsum"][:, tc:tc + 1]),
              waits=aw, tick=True)
            wxq = P.t("vector")
            for dd in range(16):
                tw = [wxq]
                if w_psbf[dd % 2] is not None:
                    tw.append(w_psbf[dd % 2])
                E("tensor", lambda e, dd=dd:
                  e.transpose(psbf[dd % 2],
                              xq_bf[:, 128 * dd:128 * (dd + 1)],
                              sb["ident_bf"]), waits=tw, tick=True)
                E("scalar", lambda e, dd=dd, tc=tc:
                  e.activation(out=xqT_all[:, dd, tc, :],
                               in_=psbf[dd % 2], func=AF.Copy),
                  waits=[P.t("tensor")], tick=True)
                w_psbf[dd % 2] = P.t("scalar")
            w_xqT_used = P.t("tensor")
            w_x1_free[ti] = (P.t("vector"), P.t("scalar"), P.d("sync"))
        w_xr_free = (P.t("vector"), P.t("scalar"))

    wldbs = [wldb, wldbB]
    w_ts_done = {}
    w_sp_done = {}
    w_a2boot = [P.t("vector"), P.t("tensor"), P.d("sync")]
    for k in range(64):
        par = k % 2
        lw = ([w_sp_done[par]] if par in w_sp_done else list(w_a2boot))
        dmov("gpsimd", wldbs[par][:, 0:HID], w2T_r[k], waits=lw)
        wld = P.d("gpsimd")
        E("vector", lambda e, par=par:
          e.tensor_scalar(out=wldbs[par][:, 0:HID],
                          in0=wldbs[par][:, 0:HID],
                          scalar1=0.0, scalar2=2.0, op0=OP.is_gt,
                          op1=OP.mult), waits=[wld], tick=True)
        w_ts_done[par] = P.t("vector")
        dmov("sync", w2s_sp.ap()[k], wldbs[par][:, 0:HID],
             waits=[w_ts_done[par]])
        w_sp_done[par] = P.d("sync")

    dmov("gpsimd", b1rep_bf, dram["b1_rep"].ap(),
         waits=[P.t("vector"), P.t("scalar")])
    dmov("gpsimd", b2rep_bf, dram["b2_rep"].ap())

    w1bufs = [w1s, w1sB]
    gts = [hq_t[:, 0:512], hgel[:, 0:512]]
    htfs = [htf, htf2]
    w_mmw1 = {}
    w_bankB1 = {}
    w_gelu = {}
    w_rd = {}
    w_g2sp = {}
    for it in range(16):
        wpar = it % 2
        lw = [w_mmw1[wpar]] if wpar in w_mmw1 else [P.t("tensor")]
        dmov("gpsimd", w1bufs[wpar], w1T_r[:, :, 512 * it:512 * (it + 1)],
             waits=lw)
        wld = P.d("gpsimd")
        E("vector", lambda e, wpar=wpar:
          e.tensor_scalar(out=w1bufs[wpar].rearrange("p a b -> p (a b)"),
                          in0=w1bufs[wpar].rearrange("p a b -> p (a b)"),
                          scalar1=0.0, scalar2=2.0, op0=OP.is_gt,
                          op1=OP.mult), waits=[wld], tick=True)
        wsg = P.t("vector")
        for tc in range(NTC):
            par = tc % 2
            pslot = psA if par == 0 else psB_
            mw = [wsg]
            if par in w_bankB1:
                mw.append(w_bankB1[par])
            if it == 0 and tc < 2:
                mw += [P.t("vector"), P.t("scalar")]
            for dd in range(16):
                E("tensor", lambda e, dd=dd, tc=tc, wpar=wpar, pslot=pslot:
                  e.matmul(pslot, xqT_all[:, dd, tc, :],
                           w1bufs[wpar][:, dd, :],
                           start=(dd == 0), stop=(dd == 15)),
                  waits=(mw if dd == 0 else ()), tick=(dd == 15))
            wh = P.t("tensor")
            hw = [wh]
            if par in w_gelu:
                hw.append(w_gelu[par])
            E("vector", lambda e, tc=tc, pslot=pslot, par=par:
              e.tensor_scalar(out=htfs[par], in0=pslot,
                              scalar1=sb["zsum"][:, tc:tc + 1],
                              scalar2=sb["ysc1_all"][:, tc:tc + 1],
                              op0=OP.subtract, op1=OP.mult),
              waits=hw, tick=True)
            w_bankB1[par] = P.t("vector")
            E("vector", lambda e, it=it, par=par:
              e.tensor_tensor(out=htfs[par], in0=htfs[par],
                              in1=b1rep_bf[:, 512 * it:512 * (it + 1)],
                              op=OP.add), tick=True)
            whf = P.t("vector")
            gw = [whf]
            if par in w_g2sp:
                gw += [w_g2sp[par], w_rd[par]]
            E("scalar", lambda e, par=par:
              e.activation(out=gts[par], in_=htfs[par], func=AF.Gelu),
              waits=gw, tick=True)
            w_gelu[par] = P.t("scalar")
            tkan = "tka" if par == 0 else "tka1"
            tkbn = "tkb" if par == 0 else "tkb1"
            E("vector", lambda e, par=par, tkan=tkan:
              e.tensor_reduce(sb[tkan], gts[par], AX.X, OP.max,
                              apply_absolute_value=True),
              waits=[P.t("scalar")], tick=True)
            w_rd[par] = P.t("vector")
            if it == 0:
                E("vector", lambda e, tc=tc, tkan=tkan:
                  e.tensor_copy(out=sb["habs8"][:, tc:tc + 1],
                                in_=sb[tkan]), tick=True)
            else:
                E("vector", lambda e, tc=tc, tkan=tkan:
                  e.tensor_tensor(out=sb["habs8"][:, tc:tc + 1],
                                  in0=sb["habs8"][:, tc:tc + 1],
                                  in1=sb[tkan], op=OP.max), tick=True)
            wgt = P.t("vector")
            E("scalar", lambda e, par=par, tkbn=tkbn:
              e.activation(out=ytmp[:, 0:512] if par == 0
                           else ytmp[:, 512:1024], in_=gts[par],
                           func=AF.Square, accum_out=sb[tkbn]),
              waits=[wgt], tick=True)
            if it == 0:
                E("vector", lambda e, tc=tc, tkbn=tkbn:
                  e.tensor_copy(out=sb["hssq8"][:, tc:tc + 1],
                                in_=sb[tkbn]), waits=[P.t("scalar")],
                  tick=True)
            else:
                E("vector", lambda e, tc=tc, tkbn=tkbn:
                  e.tensor_tensor(out=sb["hssq8"][:, tc:tc + 1],
                                  in0=sb["hssq8"][:, tc:tc + 1],
                                  in1=sb[tkbn], op=OP.add),
                  waits=[P.t("scalar")], tick=True)
            dmov("sync", g2_sp.ap()[tc, it], gts[par],
                 waits=[P.t("vector")])
            w_g2sp[par] = P.d("sync")
        w_mmw1[wpar] = P.t("tensor")
    w_b1 = P.d("sync")

    hstgs = [hstg, hstgB]
    w_tposeB2 = {}
    w_hstg_sp = {}
    for tc in range(NTC):
        E("vector", lambda e, tc=tc:
          e.tensor_scalar(out=sb["tka"], in0=sb["hssq8"][:, tc:tc + 1],
                          scalar1=1.0 / INNER, scalar2=1e-6, op0=OP.mult,
                          op1=OP.add), waits=[w_b1], tick=True)
        wtk = P.t("vector")
        E("scalar", lambda e:
          e.activation(out=sb["tka"], in_=sb["tka"], func=AF.Ln),
          waits=[wtk], tick=True)
        E("scalar", lambda e:
          e.activation(out=sb["tka"], in_=sb["tka"], func=AF.Exp,
                       scale=-0.5), tick=True)
        wk1 = P.t("scalar")
        E("vector", lambda e, tc=tc:
          e.tensor_scalar(out=sb["tkb"], in0=sb["habs8"][:, tc:tc + 1],
                          scalar1=sb["tka"], scalar2=ISQI, op0=OP.mult,
                          op1=OP.mult), waits=[wk1], tick=True)
        E("vector", lambda e:
          e.tensor_scalar(out=sb["tkb"], in0=sb["tkb"], scalar1=1e-5,
                          scalar2=None, op0=OP.max), tick=True)
        E("vector", lambda e, tc=tc:
          e.tensor_scalar(out=sb["ysc2_all"][:, tc:tc + 1], in0=sb["tkb"],
                          scalar1=sb["swk"][:, 1:2],
                          scalar2=1.0 / (127.0 * INNER * HID), op0=OP.mult,
                          op1=OP.mult), tick=True)
        wk2 = P.t("vector")
        E("scalar", lambda e:
          e.activation(out=sb["tkb"], in_=sb["tkb"], func=AF.Ln),
          waits=[wk2], tick=True)
        E("scalar", lambda e:
          e.activation(out=sb["tkb"], in_=sb["tkb"], func=AF.Exp,
                       scale=-1.0), tick=True)
        wk3 = P.t("scalar")
        E("vector", lambda e:
          e.tensor_scalar(out=sb["kq_c"], in0=sb["tkb"], scalar1=sb["tka"],
                          scalar2=127.0 * ISQI, op0=OP.mult, op1=OP.mult),
          waits=[wk3], tick=True)
        E("vector", lambda e:
          e.memset(sb["nhz_c"], 0.0), tick=True)
        for it in range(16):
            par = it % 2
            if par in w_tposeB2:
                lw = [w_tposeB2[par]]
            else:
                lw = [w_g2sp[par], P.t("scalar"), P.t("vector")]
            E("gpsimd", lambda e, tc=tc, it=it, par=par:
              e.dma_start(out=gts[par], in_=g2_sp.ap()[tc, it]),
              waits=lw, dma=True)
            wldg = P.d("gpsimd")
            E("vector", lambda e, par=par:
              e.tensor_scalar(out=htfs[par], in0=gts[par],
                              scalar1=sb["kq_c"], scalar2=RND, op0=OP.mult,
                              op1=OP.add), waits=[wldg], tick=True)
            E("vector", lambda e, par=par:
              e.tensor_scalar(out=htfs[par], in0=htfs[par], scalar1=-RND,
                              scalar2=127.0, op0=OP.add, op1=OP.min),
              tick=True)
            E("vector", lambda e, par=par:
              e.tensor_scalar(out=gts[par], in0=htfs[par], scalar1=-128.0,
                              scalar2=0.0, op0=OP.max, op1=OP.add,
                              accum_out=sb["tkb"]), tick=True)
            E("vector", lambda e:
              e.tensor_tensor(out=sb["nhz_c"], in0=sb["nhz_c"],
                              in1=sb["tkb"], op=OP.add), tick=True)
            wq2 = P.t("vector")
            for q in range(4):
                tw = [wq2]
                if w_psbf[q % 2] is not None:
                    tw.append(w_psbf[q % 2])
                E("tensor", lambda e, q=q, par=par:
                  e.transpose(psbf[q % 2],
                              gts[par][:, 128 * q:128 * (q + 1)],
                              sb["ident_bf"]), waits=tw, tick=True)
                cw = [P.t("tensor")]
                if q == 0 and par in w_hstg_sp:
                    cw.append(w_hstg_sp[par])
                E("scalar", lambda e, q=q, par=par:
                  e.activation(out=hstgs[par][:, 128 * q:128 * q + 128],
                               in_=psbf[q % 2], func=AF.Copy),
                  waits=cw, tick=True)
                w_psbf[q % 2] = P.t("scalar")
            w_tposeB2[par] = P.t("tensor")
            dmov("sync", hq_sp.ap()[tc][:, 4 * it:4 * it + 4, :]
                 .rearrange("p a b -> p (a b)"), hstgs[par],
                 waits=[P.t("scalar")])
            w_hstg_sp[par] = P.d("sync")
        E("vector", lambda e, tc=tc:
          e.tensor_copy(out=sb["hz_all"][:, tc:tc + 1], in_=sb["nhz_c"]),
          waits=[P.d("sync")], tick=True)
    w_passb = P.t("vector")

    w2bufs = [w2s_band, w2s_bandB]
    w2flat = [b.rearrange("p a b -> p (a b)") for b in w2bufs]
    xrsC = [xr, x1t]
    ytmps = [ytmp, ytmpB]
    statn = [("mu_c", "sq_c", "va_c", "rstd_c"),
             ("mu_c2", "sq_c2", "va_c2", "rstd_c2")]
    w_mmw2 = {}
    w_bankC = None
    w_mmhq = None
    w_out_done = {}
    for tcp in range(NTC // 2):
        tcs = (2 * tcp, 2 * tcp + 1)
        hqw = [w_mmhq] if w_mmhq is not None else [P.t("tensor")]
        for ti, tc in enumerate(tcs):
            dmov("sync", hq_stg[ti].rearrange("p a b -> p (a b)"),
                 hq_sp.ap()[tc].rearrange("p a b -> p (a b)"),
                 waits=(hqw if ti == 0 else []))
        wldh = P.d("sync")
        dmov("sync", xrsC[0], x1_sp.ap()[tcs[0]],
             waits=[P.t("vector"), P.t("scalar")])
        dmov("sync", xrsC[1], x1_sp.ap()[tcs[1]])
        wxr = P.d("sync")
        for ib in range(16):
            wpar = ib % 2
            lw = [w_mmw2[wpar]] if wpar in w_mmw2 else [P.t("tensor")]
            dmov("gpsimd", w2flat[wpar].rearrange("p (a c) -> p a c",
                                                  a=4),
                 w2s_sp.ap()[4 * ib:4 * ib + 4].transpose([1, 0, 2]),
                 waits=lw)
            wldw = P.d("gpsimd")
            for ti in range(2):
                for j in range(4):
                    for dt in range(4):
                        first = (ib == 0 and ti == 0 and j == 0 and dt == 0)
                        if first:
                            mw = [wldh, wldw]
                            if w_bankC is not None:
                                mw.append(w_bankC)
                            if tcp == 0:
                                mw += [P.t("vector"), P.t("scalar")]
                        elif ti == 0 and j == 0 and dt == 0:
                            mw = [wldw]
                        else:
                            mw = ()
                        E("tensor", lambda e, ti=ti, j=j, dt=dt, ib=ib,
                          wpar=wpar:
                          e.matmul(ps[4 * ti + dt][:, 0:512],
                                   hq_stg[ti][:, 4 * ib + j, :],
                                   w2flat[wpar][:, 2048 * j + 512 * dt:
                                                2048 * j + 512 * dt + 512],
                                   start=(ib == 0 and j == 0),
                                   stop=(ib == 15 and j == 3)),
                          waits=mw, tick=(dt == 3))
            w_mmw2[wpar] = P.t("tensor")
        wy = P.t("tensor")
        w_mmhq = wy
        for ti in range(2):
            for dt in range(4):
                sl = slice(512 * dt, 512 * (dt + 1))
                ew = [wy]
                if ti in w_out_done:
                    ew.append(w_out_done[ti])
                if tcp == 0 and dt == 0:
                    ew += [P.t("scalar"), P.t("vector")]
                E("vector", lambda e, ti=ti, dt=dt, sl=sl, tc=tcs[ti]:
                  e.tensor_scalar(out=ytmps[ti][:, sl],
                                  in0=ps[4 * ti + dt][:, 0:512],
                                  scalar1=sb["hz_all"][:, tc:tc + 1],
                                  scalar2=sb["ysc2_all"][:, tc:tc + 1],
                                  op0=OP.subtract, op1=OP.mult),
                  waits=ew, tick=True)
        w_bankC = P.t("vector")
        for ti, tc in enumerate(tcs):
            mu, sq, va, rstd = statn[ti]
            for dt in range(4):
                sl = slice(512 * dt, 512 * (dt + 1))
                E("vector", lambda e, ti=ti, sl=sl:
                  e.tensor_tensor(out=ytmps[ti][:, sl],
                                  in0=ytmps[ti][:, sl],
                                  in1=b2rep_bf[:, sl], op=OP.add),
                  tick=True)
                E("vector", lambda e, ti=ti, sl=sl:
                  e.tensor_tensor(out=ytmps[ti][:, sl],
                                  in0=ytmps[ti][:, sl],
                                  in1=xrsC[ti][:, sl], op=OP.add),
                  waits=([wxr] if ti == 0 else []), tick=True)
            wyt = P.t("vector")
            E("vector", lambda e, ti=ti, mu=mu:
              e.tensor_reduce(sb[mu], ytmps[ti], AX.X, OP.add),
              waits=[wyt], tick=True)
            E("scalar", lambda e, ti=ti, sq=sq:
              e.activation(out=xrsC[ti], in_=ytmps[ti], func=AF.Square,
                           accum_out=sb[sq]), waits=[wyt], tick=True)
            wst = P.t("scalar")
            E("vector", lambda e, mu=mu:
              e.tensor_scalar(out=sb[mu], in0=sb[mu], scalar1=1.0 / HID,
                              scalar2=None, op0=OP.mult), waits=[wst],
              tick=True)
            E("vector", lambda e, mu=mu, va=va:
              e.scalar_tensor_tensor(out=sb[va], in0=sb[mu], scalar=-1.0,
                                     in1=sb[mu], op0=OP.mult, op1=OP.mult),
              tick=True)
            E("vector", lambda e, sq=sq, va=va:
              e.scalar_tensor_tensor(out=sb[va], in0=sb[sq],
                                     scalar=1.0 / HID, in1=sb[va],
                                     op0=OP.mult, op1=OP.add), tick=True)
            E("vector", lambda e, va=va:
              e.tensor_scalar(out=sb[va], in0=sb[va], scalar1=1e-5,
                              scalar2=None, op0=OP.add), tick=True)
            wva = P.t("vector")
            E("scalar", lambda e, va=va, rstd=rstd:
              e.activation(out=sb[rstd], in_=sb[va], func=AF.Ln),
              waits=[wva], tick=True)
            E("scalar", lambda e, rstd=rstd:
              e.activation(out=sb[rstd], in_=sb[rstd], func=AF.Exp,
                           scale=-0.5), tick=True)
            wrs = P.t("scalar")
            E("vector", lambda e, ti=ti, mu=mu, rstd=rstd:
              e.tensor_scalar(out=ytmps[ti], in0=ytmps[ti],
                              scalar1=sb[mu], scalar2=sb[rstd],
                              op0=OP.subtract, op1=OP.mult), waits=[wrs],
              tick=True)
            dmov("sync", out_t.ap().rearrange("(n p) d -> n p d", p=128)[tc],
                 ytmps[ti], waits=[P.t("vector")])
            w_out_done[ti] = P.d("sync")
    wfin = P.d("sync")
    E("vector", lambda e: e.engine_nop(), waits=[wfin, P.d("gpsimd")],
      tick=True)

    return _finish(nc, P, sems_holder=None)


def _finish(nc, P, sems_holder=None):
    import contextlib
    with contextlib.ExitStack() as ctx:
        sems = {}
        for nm in ("psync", "pgpsimd", "ptensor", "pvector", "pscalar",
                   "dsync", "dgpsimd", "dscalar"):
            sems[nm] = ctx.enter_context(nc.semaphore(nm))
        block = ctx.enter_context(nc.Block())

        def runner(eng_name):
            def run(e):
                dma_done = 0
                prog = 0
                for fn, waits, inc in P.ops[eng_name]:
                    if (eng_name in ("vector", "scalar") and inc is not None
                            and inc[0] == "p" + eng_name and prog > 0):
                        e.wait_ge(sems["p" + eng_name], prog)
                    if (eng_name in ("gpsimd", "sync", "scalar")
                            and inc is not None
                            and inc[0] == "d" + eng_name and dma_done > 0):
                        e.wait_ge(sems["d" + eng_name], dma_done)
                    for snm, val in waits:
                        if val > 0:
                            e.wait_ge(sems[snm], val)
                    ins = fn(e)
                    if inc is not None and ins is not None:
                        ins.then_inc(sems[inc[0]], inc[1])
                        if inc[0] == "d" + eng_name:
                            dma_done += inc[1]
                        elif inc[0] == "p" + eng_name:
                            prog += inc[1]
            return run

        block.sync(runner("sync"))
        block.gpsimd(runner("gpsimd"))
        block.tensor(runner("tensor"))
        block.vector(runner("vector"))
        block.scalar(runner("scalar"))
    return nc


_NC_CACHE = None


def kernel(**inputs):
    global _NC_CACHE
    if _NC_CACHE is None:
        _NC_CACHE = build_nc()
    nc = _NC_CACHE
    in_maps = [_core_host_inputs(inputs, c) for c in range(NCORE)]
    res = run_bass_kernel_spmd(nc, in_maps, core_ids=list(range(NCORE)))
    out = np.concatenate([res.results[c]["out"] for c in range(NCORE)], 0)
    return out.reshape(B, S, HID).astype(np.float32)
